# version 04
# baseline (speedup 1.0000x reference)
import contextlib
import numpy as np
import concourse.bass as bass
import concourse.mybir as mybir
from concourse.bass_utils import run_bass_kernel_spmd

F32 = mybir.dt.float32
BF16 = mybir.dt.bfloat16
I32 = mybir.dt.int32
AF = mybir.ActivationFunctionType
OP = mybir.AluOpType

B, SEQ, HID = 2, 4096, 4096
HEADS, HD, BLK = 32, 128, 256
NB = SEQ // BLK
INTER = 11008
NGK = HID // 128
NGI = INTER // 128
EPS = 1e-5

T_OWN = 1024
DEBUG = False
LAST_RESULTS = None
S_PRE = 3072
NB_PRE = S_PRE // BLK
NTH = T_OWN // 512
ISL = 256
NSL = INTER // ISL
NTC = T_OWN // 128


def _simm(v):
    v &= 0xFFFFFFFF
    return v - (1 << 32) if v >= (1 << 31) else v


def _decay_tables():
    f32 = np.float32
    base = f32(1.0) / f32(2.0) ** f32(8.0 / HEADS)
    exponent = np.arange(HEADS, dtype=np.float32) + f32(1.0)
    factor = f32(1.0 - 1 / (32 - 1 + 1e-5) + 1e-5)
    slope = (base ** exponent * factor).astype(np.float32)
    r = np.arange(BLK, dtype=np.float32) + f32(1.0)
    q_decay = np.exp(-slope[:, None] * r[None, :]).astype(np.float32)
    k_decay = np.exp(-slope[:, None] * (f32(BLK) - r)[None, :]).astype(np.float32)
    diff = r[:, None] - r[None, :]
    diag = slope[:, None, None] * diff[None, :, :]
    diag = np.where(diag >= 0, -diag, -np.inf).astype(np.float32)
    diag_decay = np.exp(diag).astype(np.float32)
    block_decay = np.exp(-slope * f32(BLK)).astype(np.float32)
    return slope, q_decay, k_decay, diag_decay, block_decay


def _tile_qw_gu(qw):
    q = qw.reshape(NGK, 16, INTER // 128, 128)
    return np.ascontiguousarray(q.transpose(1, 2, 0, 3).reshape(16, INTER // 128, NGK * 128))


def _tile_qw_d(qw):
    q = qw.reshape(NGI, 16, 8, 512)
    return np.ascontiguousarray(q.transpose(1, 2, 0, 3))


def _tile_w(wt):
    K, OF = wt.shape
    t = wt.reshape(K // 128, 128, OF // 128, 128)
    return np.ascontiguousarray(t.transpose(2, 1, 0, 3))


def _prep(inputs):
    for k, n in (("L0_ffn_gate_g_idx", HID), ("L0_ffn_up_g_idx", HID),
                 ("L0_ffn_down_g_idx", INTER)):
        gi = np.asarray(inputs[k])
        assert np.array_equal(gi, np.arange(n) // 128), f"unexpected {k}"
    x = np.ascontiguousarray(np.asarray(inputs["L0_attn_hidden_states"], dtype=np.float32))
    qkv_w = np.asarray(inputs["L0_attn_qkv_w"], dtype=np.float32)
    out_w = np.asarray(inputs["L0_attn_out_w"], dtype=np.float32)
    gate_w = np.asarray(inputs["L0_attn_gate_w"], dtype=np.float32)
    norm_w = np.asarray(inputs["L0_attn_norm_w"], dtype=np.float32)

    shared = {
        "wqkv": _tile_w(np.ascontiguousarray(qkv_w.T)),
        "wgate": _tile_w(np.ascontiguousarray(gate_w.T)),
        "wout": _tile_w(np.ascontiguousarray(out_w.T)),
        "normc": np.ascontiguousarray(norm_w.reshape(32, 128).T),
        "g_qw": _tile_qw_gu(np.asarray(inputs["L0_ffn_gate_qweight"], dtype=np.int32)),
        "g_s": np.ascontiguousarray(np.asarray(inputs["L0_ffn_gate_scales"], dtype=np.float32)),
        "g_qz": np.ascontiguousarray(np.asarray(inputs["L0_ffn_gate_qzeros"], dtype=np.int32)),
        "u_qw": _tile_qw_gu(np.asarray(inputs["L0_ffn_up_qweight"], dtype=np.int32)),
        "u_s": np.ascontiguousarray(np.asarray(inputs["L0_ffn_up_scales"], dtype=np.float32)),
        "u_qz": np.ascontiguousarray(np.asarray(inputs["L0_ffn_up_qzeros"], dtype=np.int32)),
        "d_qw": _tile_qw_d(np.asarray(inputs["L0_ffn_down_qweight"], dtype=np.int32)),
        "d_s": np.ascontiguousarray(np.asarray(inputs["L0_ffn_down_scales"], dtype=np.float32)),
        "d_qz": np.ascontiguousarray(np.asarray(inputs["L0_ffn_down_qzeros"], dtype=np.int32)),
    }

    _, q_decay, k_decay, diag_decay, _ = _decay_tables()
    shared["decayT"] = np.ascontiguousarray(diag_decay.transpose(0, 2, 1))
    qd = q_decay.reshape(HEADS, 2, 128).transpose(2, 0, 1).reshape(128, HEADS * 2)
    kd = k_decay.reshape(HEADS, 2, 128).transpose(2, 0, 1).reshape(128, HEADS * 2)
    shared["qdec"] = np.ascontiguousarray(qd)
    shared["kdec"] = np.ascontiguousarray(kd)
    j = np.arange(128) % 8
    shared["maskv"] = ((np.int64(15) << (4 * j)).astype(np.int64)
                       .astype(np.uint32).view(np.int32).reshape(128, 1).copy())
    shared["pow2v"] = (2.0 ** (-4.0 * j)).astype(np.float32).reshape(128, 1).copy()
    ind = np.zeros((128, 2 * NGI - 1), dtype=np.float32)
    ind[:, NGI - 1] = 1.0
    shared["ind"] = ind

    per_core = []
    for c in range(8):
        s, p = c // 4, c % 4
        S = p * T_OWN
        x_seq = x[s]
        m = dict(shared)
        m["x_own"] = np.ascontiguousarray(x_seq[S:S + T_OWN])
        x_pre = np.zeros((S_PRE, HID), dtype=np.float32)
        if S > 0:
            x_pre[S_PRE - S:] = x_seq[:S]
        m["x_pre"] = x_pre
        per_core.append(m)
    return per_core


class Trk:
    def __init__(self, stack, nc, name):
        self.h = stack.enter_context(nc.semaphore(name))
        self.v = 0
        self.name = name

    def inc(self, instr, n=1):
        instr.then_inc(self.h, n)
        self.v += n
        return self.v


def build_nc():
    _, _, _, _, bdec = _decay_tables()
    nc = bass.Bass("TRN2", target_bir_lowering=False, debug=False)
    dt = lambda n, s, d, k="ExternalInput": nc.dram_tensor(n, s, d, kind=k).ap()

    x_own = dt("x_own", [T_OWN, HID], F32)
    x_pre = dt("x_pre", [S_PRE, HID], F32)
    wqkv = dt("wqkv", [96, 128, 32, 128], F32)
    wgate = dt("wgate", [32, 128, 32, 128], F32)
    wout = dt("wout", [32, 128, 32, 128], F32)
    normc = dt("normc", [128, 32], F32)
    g_qw = dt("g_qw", [16, INTER // 128, NGK * 128], I32)
    g_s = dt("g_s", [NGK, INTER], F32)
    g_qz = dt("g_qz", [NGK, INTER // 8], I32)
    u_qw = dt("u_qw", [16, INTER // 128, NGK * 128], I32)
    u_s = dt("u_s", [NGK, INTER], F32)
    u_qz = dt("u_qz", [NGK, INTER // 8], I32)
    d_qw = dt("d_qw", [16, 8, NGI, 512], I32)
    d_s = dt("d_s", [NGI, HID], F32)
    d_qz = dt("d_qz", [NGI, HID // 8], I32)
    decayT = dt("decayT", [HEADS, BLK, BLK], F32)
    qdec = dt("qdec", [128, HEADS * 2], F32)
    kdec = dt("kdec", [128, HEADS * 2], F32)
    maskv = dt("maskv", [128, 1], I32)
    pow2v = dt("pow2v", [128, 1], F32)
    ind = dt("ind", [128, 2 * NGI - 1], F32)
    y = dt("y", [T_OWN, HID], F32, "ExternalOutput")
    dbg = {}
    if DEBUG:
        dbg["d_xn"] = dt("d_xn", [T_OWN, HID], BF16, "ExternalOutput")
        dbg["d_qkv"] = dt("d_qkv", [96 * 128, T_OWN], BF16, "ExternalOutput")
        dbg["d_attn"] = dt("d_attn", [T_OWN, HID], BF16, "ExternalOutput")
        dbg["d_gate"] = dt("d_gate", [HID, T_OWN], BF16, "ExternalOutput")
        dbg["d_h"] = dt("d_h", [INTER, T_OWN], BF16, "ExternalOutput")
        dbg["d_sg"] = dt("d_sg", [HID, T_OWN], BF16, "ExternalOutput")
        dbg["d_rstd"] = dt("d_rstd", [T_OWN, 1], F32, "ExternalOutput")

    si = lambda n, s, d: nc.dram_tensor(n, s, d, kind="Internal").ap()
    xn_own_sc = si("xn_own_sc", [T_OWN, HID], BF16)
    xn_pre_sc = si("xn_pre_sc", [S_PRE, HID], BF16)
    qkv_sc = si("qkv_sc", [96 * 128, SEQ], BF16)
    attn_sc = si("attn_sc", [T_OWN, HID], BF16)
    gate_sc = si("gate_sc", [HID, T_OWN], BF16)
    h_sc = si("h_sc", [INTER, T_OWN], BF16)
    sg_sc = si("sg_sc", [HID, T_OWN], BF16)
    rstd_sc = si("rstd_sc", [T_OWN, 1], F32)
    gsb_sc = si("gsb_sc", [NGK, INTER], BF16)
    usb_sc = si("usb_sc", [NGK, INTER], BF16)
    dsb_sc = si("dsb_sc", [NGI, HID], BF16)
    szg_sc = si("szg_sc", [NGK, INTER], BF16)
    szu_sc = si("szu_sc", [NGK, INTER], BF16)

    stack = contextlib.ExitStack()
    with stack:
        sb = lambda n, s, d, **kw: stack.enter_context(nc.sbuf_tensor(n, s, d, **kw))
        psb = [stack.enter_context(nc.psum_tensor(f"ps{i}", [128, 512], F32))
               for i in range(8)]

        t_qdec = sb("t_qdec", [128, HEADS * 2], F32)
        t_kdec = sb("t_kdec", [128, HEADS * 2], F32)
        t_mask = sb("t_mask", [128, 1], I32)
        t_pow2 = sb("t_pow2", [128, 1], F32)
        t_one = sb("t_one", [1, 128], BF16)
        t_norm = sb("t_norm", [128, 32], F32)
        t_ind = sb("t_ind", [128, 2 * NGI - 1], BF16)
        t_szd = sb("t_szd", [NGI, HID], BF16)
        t_const = sb("t_const", [128, 2], F32)

        s_wl = Trk(stack, nc, "s_wl")
        s_al = Trk(stack, nc, "s_al")
        s_tl = Trk(stack, nc, "s_tl")
        s_st = Trk(stack, nc, "s_st")
        s_pe = Trk(stack, nc, "s_pe")
        s_act = Trk(stack, nc, "s_act")
        s_dve = Trk(stack, nc, "s_dve")
        s_pl = Trk(stack, nc, "s_pl")
        s_out = Trk(stack, nc, "s_out")
        ALL_TRK = [s_wl, s_al, s_tl, s_st, s_pe, s_act, s_dve, s_pl, s_out]

        wait_cache = {}

        def W(eng, trk, v=None):
            v = trk.v if v is None else v
            if v <= 0:
                return
            key = (id(eng), trk.name)
            if wait_cache.get(key, 0) >= v:
                return
            eng.wait_ge(trk.h, v)
            wait_cache[key] = v

        def dma(eng, out, in_, trk, **kw):
            eng.dma_start(out=out, in_=in_, **kw).then_inc(trk.h, 16)
            trk.v += 16
            return trk.v

        SY, SC, VE, PE, GP = nc.sync, nc.scalar, nc.vector, nc.tensor, nc.gpsimd
        ENGINES = [SY, SC, VE, PE, GP]

        def barrier():
            for e in ENGINES:
                for t in ALL_TRK:
                    W(e, t)

        dma(SY, t_qdec[:, :], qdec, s_al)
        dma(SY, t_kdec[:, :], kdec, s_al)
        dma(SY, t_mask[:, :], maskv, s_al)
        dma(SY, t_pow2[:, :], pow2v, s_al)
        dma(SY, t_norm[:, :], normc, s_al)
        dma(GP, t_ind[:, :], ind, s_wl)
        scale_mark = s_wl.v
        s_dve.inc(VE.memset(t_one[:, :], 1.0))
        s_dve.inc(VE.memset(t_const[:, 0:1], EPS))
        s_dve.inc(VE.memset(t_const[:, 1:2], 1.0 / HID))

        with contextlib.ExitStack() as pre:
            sbl = lambda n, s, d: pre.enter_context(nc.sbuf_tensor(n, s, d))
            t_zi = sbl("t_zi", [NGI, INTER], I32)
            t_zf = sbl("t_zf", [NGI, INTER], F32)
            t_qzb = sbl("t_qzb", [NGI, INTER // 8], I32)
            t_sf = sbl("t_sf", [NGI, INTER], F32)
            t_szb = sbl("t_szb", [NGI, INTER], BF16)
            for (qz_d, s_d, spill_to, ng, width) in (
                (g_qz, g_s, szg_sc, NGK, INTER),
                (u_qz, u_s, szu_sc, NGK, INTER),
                (d_qz, d_s, None, NGI, HID),
            ):
                w8 = width // 8
                W(SY, s_dve)
                W(SY, s_st)
                dma(SY, t_qzb[0:ng, 0:w8], qz_d, s_al)
                dma(SY, t_sf[0:ng, 0:width], s_d, s_al)
                W(VE, s_al)
                for jn in range(8):
                    VE.tensor_scalar(t_zi[0:ng, jn:width:8], t_qzb[0:ng, 0:w8],
                                     _simm(15 << (4 * jn)), None, OP.bitwise_and)
                for jn in range(8):
                    VE.tensor_scalar(t_zf[0:ng, jn:width:8], t_zi[0:ng, jn:width:8],
                                     float(2.0 ** (-4 * jn)), None, OP.mult)
                VE.tensor_scalar(t_zf[0:ng, 0:width], t_zf[0:ng, 0:width],
                                 1.0, None, OP.add)
                tgt = t_szd if spill_to is None else t_szb
                iv = VE.scalar_tensor_tensor(tgt[0:ng, 0:width], t_zf[0:ng, 0:width],
                                             -1.0, t_sf[0:ng, 0:width],
                                             OP.mult, OP.mult)
                s_dve.inc(iv)
                if spill_to is not None:
                    W(SY, s_dve)
                    dma(SY, spill_to, t_szb[0:ng, 0:width], s_st)
            barrier()
        sz_spill_mark = s_st.v

        NTILE = NTC + S_PRE // 128
        with contextlib.ExitStack() as p0:
            sbl = lambda n, s, d, **k: p0.enter_context(nc.sbuf_tensor(n, s, d, **k))
            xb = [sbl(f"p0x{i}", [128, HID], F32) for i in range(2)]
            dummy = sbl("p0dummy", [128, HID], F32)
            xnb = [sbl(f"p0n{i}", [128, HID], BF16) for i in range(2)]
            st_sum = sbl("p0sum", [128, NTILE], F32)
            st_ss = sbl("p0ss", [128, NTILE], F32)
            st_t = sbl("p0t", [128, NTILE], F32)
            tiles = [(x_own[t * 128:(t + 1) * 128, :],
                      xn_own_sc[t * 128:(t + 1) * 128, :]) for t in range(NTC)]
            tiles += [(x_pre[t * 128:(t + 1) * 128, :],
                       xn_pre_sc[t * 128:(t + 1) * 128, :]) for t in range(S_PRE // 128)]
            act_war = [0, 0]
            for i, (src, dst) in enumerate(tiles):
                b = i % 2
                W(SY, s_act, act_war[b])
                dma(SY, xb[b][:, :], src, s_al)
                W(SC, s_al)
                SC.activation(dummy[:, :], xb[b][:, :], AF.Copy,
                              accum_out=st_sum[:, i:i + 1])
                i1 = SC.activation(dummy[:, :], xb[b][:, :], AF.Square,
                                   accum_out=st_ss[:, i:i + 1])
                act_war[b] = s_act.inc(i1)
            W(VE, s_act)
            iv = VE.tensor_scalar(st_sum[:, :], st_sum[:, :], -1.0 / HID, None,
                                  OP.mult)
            s_dve.inc(iv)
            W(VE, s_dve)
            iv = VE.tensor_tensor(st_t[:, :], st_sum[:, :], st_sum[:, :], OP.mult)
            s_dve.inc(iv)
            W(VE, s_dve)
            iv = VE.scalar_tensor_tensor(st_ss[:, :], st_ss[:, :], 1.0 / HID,
                                         st_t[:, :], OP.mult, OP.subtract)
            s_dve.inc(iv)
            W(SC, s_dve)
            i3 = SC.activation(st_ss[:, :], st_ss[:, :], AF.Sqrt,
                               bias=t_const[:, 0:1])
            s_act.inc(i3)
            W(VE, s_act)
            iv = VE.reciprocal(st_ss[:, :], st_ss[:, :])
            s_dve.inc(iv)
            W(VE, s_dve)
            nrm_war = [0, 0]
            sp_war = [0, 0]
            for i, (src, dst) in enumerate(tiles):
                b = i % 2
                W(SY, s_dve, nrm_war[b])
                dma(SY, xb[b][:, :], src, s_al)
                W(VE, s_al)
                W(VE, s_st, sp_war[b])
                iv = VE.tensor_scalar(xnb[b][:, :], xb[b][:, :],
                                      st_sum[:, i:i + 1], st_ss[:, i:i + 1],
                                      OP.add, OP.mult)
                nrm_war[b] = s_dve.inc(iv)
                W(SY, s_dve)
                sp_war[b] = dma(SY, dst, xnb[b][:, :], s_st)
            barrier()
        p0_spill_done = s_st.v

        kv_chunks = [c for c in range(96) if c % 3 != 0]
        FH_T = S_PRE // 2
        FH_TILES = FH_T // 512

        pstate = {"war_pe": [0, 0], "war_act": [0, 0], "war_sp": [0, 0]}

        def proj_pass(wb, evb, xn_tile_ap, ntile, chunk_list, wdram, act_fn,
                      spill_fn, bank_groups):
            for idx, ofc in enumerate(chunk_list):
                bwt = idx % 2
                W(GP, s_pe, pstate["war_pe"][bwt])
                dma(GP, wb[bwt][:, :], wdram[ofc], s_wl)
                g = idx % 2
                banks = bank_groups[g]
                W(PE, s_wl)
                W(PE, s_act, pstate["war_act"][g])
                mm = None
                for c in range(32):
                    for t in range(ntile):
                        mm = PE.matmul(psb[banks[t]][:, :],
                                       wb[bwt][:, c * 128:(c + 1) * 128],
                                       xn_tile_ap(c, t),
                                       start=(c == 0), stop=(c == 31))
                pstate["war_pe"][bwt] = s_pe.inc(mm)
                W(SC, s_pe)
                W(SC, s_st, pstate["war_sp"][g])
                ia = None
                for t in range(ntile):
                    ia = SC.activation(evb[g][:, t * 512:(t + 1) * 512],
                                       psb[banks[t]][:, :], act_fn)
                pstate["war_act"][g] = s_act.inc(ia)
                W(SY, s_act)
                pstate["war_sp"][g] = dma(SY, spill_fn(ofc, ntile),
                                          evb[g][:, 0:ntile * 512], s_st)

        st14 = contextlib.ExitStack()
        with contextlib.ExitStack() as p1:
            sbl = lambda n, s, d, **k: p1.enter_context(nc.sbuf_tensor(n, s, d, **k))
            wb = [sbl(f"p1w{i}", [128, 4096], BF16) for i in range(2)]
            evb = [sbl(f"p1e{i}", [128, 3 * 512], BF16) for i in range(2)]

            with contextlib.ExitStack() as p1f:
                xnf = p1f.enter_context(
                    nc.sbuf_tensor("p1xnf", [128, 32 * FH_T], BF16,
                                   is_transpose_dest=True))
                for fh in range(2):
                    W(SC, s_st, p0_spill_done)
                    if fh > 0:
                        W(SC, s_pe)
                    for tc in range(FH_T // 128):
                        for c in range(32):
                            dma(SC, xnf[:, c * FH_T + tc * 128:c * FH_T + (tc + 1) * 128],
                                xn_pre_sc[fh * FH_T + tc * 128:fh * FH_T + (tc + 1) * 128,
                                          c * 128:(c + 1) * 128],
                                s_tl, transpose=True)
                    W(PE, s_tl)
                    col0 = fh * FH_T

                    def spill_kv(ofc, ntile, col0=col0):
                        return qkv_sc[ofc * 128:(ofc + 1) * 128,
                                      col0:col0 + ntile * 512]

                    proj_pass(wb, evb,
                              lambda c, t: xnf[:, c * FH_T + t * 512:c * FH_T + (t + 1) * 512],
                              FH_TILES, kv_chunks, wqkv, AF.Silu, spill_kv,
                              [(0, 1, 2), (3, 4, 5)])

            xno = st14.enter_context(
                nc.sbuf_tensor("xno", [128, 32 * T_OWN], BF16,
                               is_transpose_dest=True, side="right"))
            W(SC, s_st, p0_spill_done)
            W(SC, s_pe)
            for tc in range(NTC):
                for c in range(32):
                    dma(SC, xno[:, c * T_OWN + tc * 128:c * T_OWN + (tc + 1) * 128],
                        xn_own_sc[tc * 128:(tc + 1) * 128, c * 128:(c + 1) * 128],
                        s_tl, transpose=True)
            W(PE, s_tl)

            def spill_own(ofc, ntile):
                return qkv_sc[ofc * 128:(ofc + 1) * 128, S_PRE:S_PRE + ntile * 512]

            proj_pass(wb, evb,
                      lambda c, t: xno[:, c * T_OWN + t * 512:c * T_OWN + (t + 1) * 512],
                      NTH, list(range(96)), wqkv, AF.Silu, spill_own,
                      [(0, 1), (2, 3)])
        barrier()

        with contextlib.ExitStack() as p2:
            sbl = lambda n, s, d, **k: p2.enter_context(nc.sbuf_tensor(n, s, d, **k))
            sdt = sbl("sdt", [128, HEADS * 2 * 256], F32)
            lkv = sbl("lkv", [128, HEADS * 128], F32)
            lkv16 = sbl("lkv16", [128, HEADS * 128], BF16)
            mq = [sbl(f"mq{i}", [128, 256], BF16) for i in range(2)]
            mkf = [sbl(f"mkf{i}", [128, 256], BF16) for i in range(2)]
            mkt = [sbl(f"mkt{i}", [128, 256], BF16, is_transpose_dest=True)
                   for i in range(2)]
            mvt = [sbl(f"mvt{i}", [128, 256], BF16, is_transpose_dest=True)
                   for i in range(2)]
            khat = [sbl(f"khat{i}", [128, 256], BF16) for i in range(2)]
            scd = [sbl(f"scd{i}", [128, 512], BF16) for i in range(2)]
            t1b = [sbl(f"t1b{i}", [128, 256], F32) for i in range(2)]
            attn_blk = sbl("attn_blk", [128, 2 * HID], BF16)

            for h in range(HEADS):
                for half in range(2):
                    dma(SY, sdt[:, (h * 2 + half) * 256:(h * 2 + half + 1) * 256],
                        decayT[h, half * 128:(half + 1) * 128, :], s_al)
            VE.memset(lkv[:, :], 0.0)
            s_dve.inc(VE.memset(lkv16[:, :], 0.0))

            unit_dve = {}
            pe_done = {}
            dve_done = {}
            attn_war = 0
            for n in range(NB):
                owned = n >= NB_PRE
                for h in range(HEADS):
                    uidx = n * HEADS + h
                    u = uidx % 2
                    prev_pe = pe_done.get(uidx - 2, 0)
                    prev_dve = dve_done.get(uidx - 2, 0)
                    W(SY, s_pe, prev_pe)
                    W(SC, s_pe, prev_pe)
                    if owned:
                        dma(SY, mq[u][:, :],
                            qkv_sc[(3 * h) * 128:(3 * h + 1) * 128,
                                   n * 256:(n + 1) * 256], s_al)
                        dma(SY, mkf[u][:, :],
                            qkv_sc[(3 * h + 1) * 128:(3 * h + 2) * 128,
                                   n * 256:(n + 1) * 256], s_al)
                    al_mark = s_al.v
                    for half in range(2):
                        dma(SC, mkt[u][:, half * 128:(half + 1) * 128],
                            qkv_sc[(3 * h + 1) * 128:(3 * h + 2) * 128,
                                   n * 256 + half * 128:n * 256 + (half + 1) * 128],
                            s_tl, transpose=True)
                        dma(SC, mvt[u][:, half * 128:(half + 1) * 128],
                            qkv_sc[(3 * h + 2) * 128:(3 * h + 3) * 128,
                                   n * 256 + half * 128:n * 256 + (half + 1) * 128],
                            s_tl, transpose=True)
                    tl_mark = s_tl.v
                    W(VE, s_tl, tl_mark)
                    W(VE, s_pe, prev_pe)
                    iv = None
                    for half in range(2):
                        iv = VE.tensor_scalar(
                            khat[u][:, half * 128:(half + 1) * 128],
                            mkt[u][:, half * 128:(half + 1) * 128],
                            t_kdec[:, h * 2 + half:h * 2 + half + 1], None, OP.mult)
                    khat_mark = s_dve.inc(iv)

                    bs, bo, bi, bu = (0, 1, 2, 3) if u == 0 else (4, 5, 6, 7)
                    if owned:
                        W(PE, s_al, al_mark)
                        W(PE, s_dve, prev_dve)
                        ip = None
                        for half in range(2):
                            ip = PE.matmul(psb[bs][:, half * 256:(half + 1) * 256],
                                           mkf[u][:, half * 128:(half + 1) * 128],
                                           mq[u][:, :], start=True, stop=True)
                        sc_mark = s_pe.inc(ip)
                        W(VE, s_pe, sc_mark)
                        ivv = None
                        for half in range(2):
                            ivv = VE.tensor_tensor(
                                scd[u][:, half * 256:(half + 1) * 256],
                                psb[bs][:, half * 256:(half + 1) * 256],
                                sdt[:, (h * 2 + half) * 256:(h * 2 + half + 1) * 256],
                                OP.mult)
                        scd_mark = s_dve.inc(ivv)
                        W(PE, s_dve, scd_mark)
                        W(PE, s_dve, unit_dve.get((n - 1, h), 0))
                        for tqh in range(2):
                            PE.matmul(psb[bo][:, tqh * 128:(tqh + 1) * 128],
                                      scd[u][:, tqh * 128:tqh * 128 + 128],
                                      mvt[u][:, 0:128], start=True, stop=False)
                            PE.matmul(psb[bo][:, tqh * 128:(tqh + 1) * 128],
                                      scd[u][:, 256 + tqh * 128:256 + tqh * 128 + 128],
                                      mvt[u][:, 128:256], start=False, stop=True)
                            PE.matmul(psb[bi][:, tqh * 128:(tqh + 1) * 128],
                                      mq[u][:, tqh * 128:(tqh + 1) * 128],
                                      lkv16[:, h * 128:(h + 1) * 128],
                                      start=True, stop=True)
                    W(PE, s_dve, khat_mark)
                    W(PE, s_dve, prev_dve)
                    PE.matmul(psb[bu][:, 0:128], khat[u][:, 0:128],
                              mvt[u][:, 0:128], start=True, stop=False)
                    ip2 = PE.matmul(psb[bu][:, 0:128], khat[u][:, 128:256],
                                    mvt[u][:, 128:256], start=False, stop=True)
                    pe_done[uidx] = s_pe.inc(ip2)
                    W(VE, s_pe, pe_done[uidx])
                    if owned:
                        W(VE, s_st, attn_war)
                        it1 = None
                        for tqh in range(2):
                            it1 = VE.tensor_scalar(
                                t1b[u][:, tqh * 128:(tqh + 1) * 128],
                                psb[bi][:, tqh * 128:(tqh + 1) * 128],
                                t_qdec[:, h * 2 + tqh:h * 2 + tqh + 1], None, OP.mult)
                        s_dve.inc(it1)
                        W(VE, s_dve)
                        for tqh in range(2):
                            VE.tensor_tensor(
                                attn_blk[:, tqh * HID + h * 128:tqh * HID + (h + 1) * 128],
                                psb[bo][:, tqh * 128:(tqh + 1) * 128],
                                t1b[u][:, tqh * 128:(tqh + 1) * 128], OP.add)
                    ist = VE.scalar_tensor_tensor(lkv[:, h * 128:(h + 1) * 128],
                                                  lkv[:, h * 128:(h + 1) * 128],
                                                  float(bdec[h]),
                                                  psb[bu][:, 0:128],
                                                  OP.mult, OP.add)
                    s_dve.inc(ist)
                    W(VE, s_dve)
                    iu = VE.tensor_copy(lkv16[:, h * 128:(h + 1) * 128],
                                        lkv[:, h * 128:(h + 1) * 128])
                    dve_done[uidx] = s_dve.inc(iu)
                    unit_dve[(n, h)] = dve_done[uidx]
                if owned:
                    W(SY, s_dve)
                    r0 = (n - NB_PRE) * 256
                    for tqh in range(2):
                        attn_war = dma(
                            SY, attn_sc[r0 + tqh * 128:r0 + (tqh + 1) * 128, :],
                            attn_blk[:, tqh * HID:(tqh + 1) * HID], s_st)
        barrier()

        with contextlib.ExitStack() as p3:
            sbl = lambda n, s, d, **k: p3.enter_context(nc.sbuf_tensor(n, s, d, **k))
            wb = [sbl(f"p3w{i}", [128, 4096], BF16) for i in range(2)]
            evb = [sbl(f"p3e{i}", [128, 2 * 512], BF16) for i in range(2)]
            pstate["war_pe"] = [0, 0]
            pstate["war_act"] = [0, 0]
            pstate["war_sp"] = [0, 0]

            def spill_gate(ofc, ntile):
                return gate_sc[ofc * 128:(ofc + 1) * 128, 0:ntile * 512]

            proj_pass(wb, evb,
                      lambda c, t: xno[:, c * T_OWN + t * 512:c * T_OWN + (t + 1) * 512],
                      NTH, list(range(32)), wgate, AF.Sigmoid, spill_gate,
                      [(0, 1), (2, 3)])
        barrier()

        P4SL = 128
        P4N = INTER // P4SL
        t_hg = sb("t_hg", [NGI, T_OWN], BF16)
        with contextlib.ExitStack() as p4:
            sbl = lambda n, s, d, **k: p4.enter_context(nc.sbuf_tensor(n, s, d, **k))
            t_xg = sbl("t_xg", [NGK, T_OWN], BF16)
            rep_all = {m: sbl(f"p4r{m}", [128, NGK * P4SL], I32) for m in "gu"}
            sbc_all = {m: sbl(f"p4b{m}", [128, NGK * P4SL], F32) for m in "gu"}
            dq = {m: [sbl(f"p4d{m}{i}", [128, NGK * P4SL], BF16) for i in range(2)]
                  for m in "gu"}
            szb = {m: [sbl(f"p4z{m}{i}", [NGK, P4SL], BF16) for i in range(2)]
                   for m in "gu"}
            h_t = [sbl(f"p4h{i}", [128, T_OWN], BF16) for i in range(2)]
            sil = [sbl(f"p4sil{i}", [128, 512], BF16) for i in range(2)]

            for th in range(NTH):
                mm = None
                for c in range(32):
                    mm = PE.matmul(psb[6 + th][0:NGK, :],
                                   t_ind[:, (NGI - 1 - c):(NGI - 1 - c) + NGK],
                                   xno[:, c * T_OWN + th * 512:c * T_OWN + (th + 1) * 512],
                                   start=(c == 0), stop=(c == 31))
                s_pe.inc(mm)
            W(SC, s_pe)
            ia = None
            for th in range(NTH):
                ia = SC.activation(t_xg[0:NGK, th * 512:(th + 1) * 512],
                                   psb[6 + th][0:NGK, :], AF.Copy)
            xg_mark = s_act.inc(ia)
            W(PE, s_act, xg_mark)

            qw_d = {"g": g_qw, "u": u_qw}
            sc_d = {"g": g_s, "u": u_s}
            sz_sc = {"g": szg_sc, "u": szu_sc}

            stage_done = [0] * P4N
            sz_mark = [0] * P4N
            and_done = {m: 0 for m in "gu"}
            stt_done = {m: 0 for m in "gu"}
            dq_war = [0, 0]
            szb_war = [0, 0]
            h_spill_war = [0, 0]
            hg_war = [0, 0]
            sil_war = [0, 0]

            def stage(j):
                sj = j % 2
                c0, c1 = j * P4SL, (j + 1) * P4SL
                for m in "gu":
                    W(SY, s_dve, stt_done[m])
                    dma(SY, rep_all[m][:, :],
                        qw_d[m][:, j, :].unsqueeze(1)
                        .broadcast_to([16, 8, NGK * P4SL]), s_al)
                    W(SY, s_dve, stt_done[m])
                    dma(SY, sbc_all[m][:, :],
                        sc_d[m][:, c0:c1].unsqueeze(0)
                        .broadcast_to([128, NGK, P4SL]), s_al)
                    W(SY, s_pe, szb_war[sj])
                    W(SY, s_st, sz_spill_mark)
                    dma(SY, szb[m][sj][0:NGK, :], sz_sc[m][:, c0:c1], s_al)
                ld_mark = s_al.v
                for m in "gu":
                    W(VE, s_al, ld_mark)
                    W(VE, s_pe, dq_war[sj])
                    ig = VE.tensor_scalar(rep_all[m][:, :], rep_all[m][:, :],
                                          t_mask[:, 0:1], None, OP.bitwise_and)
                    and_done[m] = s_dve.inc(ig)
                    im = VE.scalar_tensor_tensor(dq[m][sj][:, :], rep_all[m][:, :],
                                                 t_pow2[:, 0:1], sbc_all[m][:, :],
                                                 OP.mult, OP.mult)
                    stt_done[m] = s_dve.inc(im)
                stage_done[j] = s_dve.v
                sz_mark[j] = s_al.v

            def main_th(j, th):
                sj = j % 2
                gi = j
                W(PE, s_dve, stage_done[j])
                W(PE, s_al, sz_mark[j])
                for g in range(NGK):
                    PE.matmul(psb[th][:, :],
                              dq["g"][sj][:, g * P4SL:(g + 1) * P4SL],
                              xno[:, g * T_OWN + th * 512:g * T_OWN + (th + 1) * 512],
                              start=(g == 0), stop=False)
                    PE.matmul(psb[2 + th][:, :],
                              dq["u"][sj][:, g * P4SL:(g + 1) * P4SL],
                              xno[:, g * T_OWN + th * 512:g * T_OWN + (th + 1) * 512],
                              start=(g == 0), stop=False)
                W(PE, s_act, xg_mark)
                PE.matmul(psb[th][:, :], szb["g"][sj][0:NGK, :],
                          t_xg[0:NGK, th * 512:(th + 1) * 512],
                          start=False, stop=True)
                mmu = PE.matmul(psb[2 + th][:, :], szb["u"][sj][0:NGK, :],
                                t_xg[0:NGK, th * 512:(th + 1) * 512],
                                start=False, stop=True)
                gu_mark = s_pe.inc(mmu)
                W(SC, s_pe, gu_mark)
                W(SC, s_dve, sil_war[th])
                isl_ = SC.activation(sil[th][:, :], psb[th][:, :], AF.Silu)
                smark = s_act.inc(isl_)
                hb = gi % 2
                W(VE, s_act, smark)
                W(VE, s_st, h_spill_war[hb])
                W(VE, s_pe, hg_war[hb])
                ih = VE.tensor_tensor(h_t[hb][:, th * 512:(th + 1) * 512],
                                      sil[th][:, :], psb[2 + th][:, :], OP.mult)
                h_mark = s_dve.inc(ih)
                sil_war[th] = h_mark
                W(PE, s_dve, h_mark)
                ihg = PE.matmul(psb[6 + th][0:NGI, :],
                                t_ind[:, (NGI - 1 - gi):(NGI - 1 - gi) + NGI],
                                h_t[hb][:, th * 512:(th + 1) * 512],
                                start=(gi == 0), stop=(gi == P4N - 1))
                hg_war[hb] = s_pe.inc(ihg)

            stage(0)
            for j in range(P4N):
                gi = j
                hb = gi % 2
                main_th(j, 0)
                if j + 1 < P4N:
                    stage(j + 1)
                main_th(j, 1)
                W(SY, s_dve)
                h_spill_war[hb] = dma(SY, h_sc[gi * 128:(gi + 1) * 128, :],
                                      h_t[hb][:, :], s_st)
                sj = j % 2
                dq_war[sj] = s_pe.v
                szb_war[sj] = s_pe.v
            W(SC, s_pe)
            ia = None
            for th in range(NTH):
                ia = SC.activation(t_hg[0:NGI, th * 512:(th + 1) * 512],
                                   psb[6 + th][0:NGI, :], AF.Copy)
            s_act.inc(ia)
        st14.close()
        barrier()

        st56 = contextlib.ExitStack()
        t_gfm = st56.enter_context(
            nc.sbuf_tensor("t_gfm", [128, 32 * T_OWN], BF16, side="right"))
        GB = 8
        gblocks = [(g0, min(GB, NGI - g0)) for g0 in range(0, NGI, GB)]
        with contextlib.ExitStack() as p5:
            sbl = lambda n, s, d, **k: p5.enter_context(nc.sbuf_tensor(n, s, d, **k))
            repd = sbl("p5r", [128, GB * 512], I32)
            sbcd = sbl("p5s", [128, GB * 512], F32)
            dqd = [sbl(f"p5w{i}", [128, GB * 512], BF16) for i in range(2)]
            hb = [sbl(f"p5h{i}", [128, GB * T_OWN], BF16) for i in range(2)]
            stt5 = 0
            dq_war5 = [0, 0]
            h_war5 = [0, 0]
            for og in range(8):
                c0 = og * 512
                for bi, (g0, gn) in enumerate(gblocks):
                    bb = bi % 2
                    W(SY, s_dve, stt5)
                    dma(SY, repd[:, 0:gn * 512],
                        d_qw[:, og, g0:g0 + gn, :].unsqueeze(1)
                        .broadcast_to([16, 8, gn, 512]), s_al)
                    dma(SY, sbcd[:, 0:gn * 512],
                        d_s[g0:g0 + gn, c0:c0 + 512].unsqueeze(0)
                        .broadcast_to([128, gn, 512]), s_al)
                    W(SY, s_pe, h_war5[bb])
                    dma(SY, hb[bb][:, 0:gn * T_OWN],
                        h_sc[g0 * 128:(g0 + gn) * 128, :]
                        .rearrange("(g p) t -> p g t", p=128), s_al)
                    ld5 = s_al.v
                    W(VE, s_al, ld5)
                    W(VE, s_pe, dq_war5[bb])
                    VE.tensor_scalar(repd[:, 0:gn * 512], repd[:, 0:gn * 512],
                                     t_mask[:, 0:1], None, OP.bitwise_and)
                    im = VE.scalar_tensor_tensor(dqd[bb][:, 0:gn * 512],
                                                 repd[:, 0:gn * 512],
                                                 t_pow2[:, 0:1],
                                                 sbcd[:, 0:gn * 512],
                                                 OP.mult, OP.mult)
                    stt5 = s_dve.inc(im)
                    W(PE, s_dve, stt5)
                    W(PE, s_al, ld5)
                    if og > 0 and bi == 0:
                        W(PE, s_act)
                    mm = None
                    for g in range(gn):
                        for ofc in range(4):
                            for th in range(NTH):
                                mm = PE.matmul(
                                    psb[ofc * 2 + th][:, :],
                                    dqd[bb][:, g * 512 + ofc * 128:g * 512 + (ofc + 1) * 128],
                                    hb[bb][:, g * T_OWN + th * 512:g * T_OWN + (th + 1) * 512],
                                    start=(g0 + g == 0), stop=False)
                    pe5 = s_pe.inc(mm)
                    dq_war5[bb] = pe5
                    h_war5[bb] = pe5
                mm = None
                for ofc in range(4):
                    for th in range(NTH):
                        mm = PE.matmul(psb[ofc * 2 + th][:, :],
                                       t_szd[0:NGI, c0 + ofc * 128:c0 + (ofc + 1) * 128],
                                       t_hg[0:NGI, th * 512:(th + 1) * 512],
                                       start=False, stop=True)
                dn_mark = s_pe.inc(mm)
                W(SC, s_pe, dn_mark)
                ia = None
                for ofc in range(4):
                    ofc_g = og * 4 + ofc
                    for th in range(NTH):
                        ia = SC.activation(
                            t_gfm[:, ofc_g * T_OWN + th * 512:ofc_g * T_OWN + (th + 1) * 512],
                            psb[ofc * 2 + th][:, :], AF.Silu)
                s_act.inc(ia)
        barrier()

        with contextlib.ExitStack() as p6:
            sbl = lambda n, s, d, **k: p6.enter_context(nc.sbuf_tensor(n, s, d, **k))
            ab2 = [sbl("p6a0", [128, HID], BF16)] * 2
            ss2 = sbl("p6ss", [128, NTC], F32)
            rr = sbl("p6rr", [128, NTC], F32)
            rrow = sbl("p6row", [1, T_OWN], BF16)
            afm = [sbl(f"p6f{i}", [128, T_OWN], BF16, is_transpose_dest=True)
                   for i in range(2)]
            gfb = [sbl(f"p6g{i}", [128, T_OWN], BF16) for i in range(2)]
            t_prod = sbl("p6p", [128, 32 * T_OWN], BF16, side="right")
            wob = [sbl(f"p6w{i}", [128, 4096], BF16) for i in range(2)]
            sgt = [sbl(f"p6sg{i}", [128, T_OWN], BF16) for i in range(2)]
            xb2 = sbl("p6x", [128, HID], F32)
            sgt2 = sbl("p6s2", [128, HID], BF16, is_transpose_dest=True)
            dummy2 = xb2

            aw_s = 0
            for tc in range(NTC):
                W(SY, s_act, aw_s)
                dma(SY, ab2[0][:, :], attn_sc[tc * 128:(tc + 1) * 128, :], s_al)
                W(SC, s_al)
                isq = SC.activation(dummy2[:, :], ab2[0][:, :], AF.Square,
                                    accum_out=ss2[:, tc:tc + 1])
                aw_s = s_act.inc(isq)
            W(SC, s_act)
            i3 = SC.activation(ss2[:, :], ss2[:, :], AF.Sqrt,
                               bias=t_const[:, 0:1], scale=t_const[:, 1:2])
            s_act.inc(i3)
            W(VE, s_act)
            iv = VE.reciprocal(rr[:, :], ss2[:, :])
            s_dve.inc(iv)
            W(SY, s_dve)
            dma(SY, rstd_sc.rearrange("(i p) one -> p (i one)", p=128), rr[:, :],
                s_st, allow_slow_non_contiguous=True)
            W(GP, s_st)
            dma(GP, rrow[0:1, :], rstd_sc.rearrange("t one -> one t"), s_wl)
            W(PE, s_wl)
            PE.matmul(psb[0][:, :], t_one[0:1, :], rrow[0:1, 0:512],
                      start=True, stop=True)
            ib = PE.matmul(psb[1][:, :], t_one[0:1, :], rrow[0:1, 512:1024],
                           start=True, stop=True)
            rb_mark = s_pe.inc(ib)

            war_p6 = [0, 0]
            for fc in range(32):
                b = fc % 2
                W(SC, s_dve, war_p6[b])
                W(SY, s_dve, war_p6[b])
                for tc in range(NTC):
                    dma(SC, afm[b][:, tc * 128:(tc + 1) * 128],
                        attn_sc[tc * 128:(tc + 1) * 128, fc * 128:(fc + 1) * 128],
                        s_tl, transpose=True)
                dma(SY, gfb[b][:, :], gate_sc[fc * 128:(fc + 1) * 128, :], s_al)
                W(VE, s_tl)
                W(VE, s_al)
                VE.tensor_tensor(afm[b][:, :], afm[b][:, :], gfb[b][:, :], OP.mult)
                W(VE, s_pe, rb_mark)
                iv = None
                for th in range(NTH):
                    iv = VE.scalar_tensor_tensor(
                        t_prod[:, fc * T_OWN + th * 512:fc * T_OWN + (th + 1) * 512],
                        afm[b][:, th * 512:(th + 1) * 512],
                        t_norm[:, fc:fc + 1],
                        psb[th][:, :], OP.mult, OP.mult)
                war_p6[b] = s_dve.inc(iv)
            prod_mark = s_dve.v

            war_w6 = [0, 0]
            war_sg = [0, 0]
            sg_spill = [0, 0]
            for ofc in range(32):
                b = ofc % 2
                W(GP, s_pe, war_w6[b])
                dma(GP, wob[b][:, :], wout[ofc], s_wl)
                banks = (2, 3) if b == 0 else (4, 5)
                W(PE, s_wl)
                W(PE, s_dve, prod_mark)
                W(PE, s_dve, war_sg[b])
                mm = None
                for c in range(32):
                    for th in range(NTH):
                        mm = PE.matmul(psb[banks[th]][:, :],
                                       wob[b][:, c * 128:(c + 1) * 128],
                                       t_prod[:, c * T_OWN + th * 512:c * T_OWN + (th + 1) * 512],
                                       start=(c == 0), stop=(c == 31))
                war_w6[b] = s_pe.inc(mm)
                W(VE, s_pe, war_w6[b])
                W(VE, s_st, sg_spill[b])
                iv = None
                for th in range(NTH):
                    iv = VE.tensor_tensor(
                        sgt[b][:, th * 512:(th + 1) * 512],
                        psb[banks[th]][:, :],
                        t_gfm[:, ofc * T_OWN + th * 512:ofc * T_OWN + (th + 1) * 512],
                        OP.mult)
                war_sg[b] = s_dve.inc(iv)
                W(SY, s_dve, war_sg[b])
                sg_spill[b] = dma(SY, sg_sc[ofc * 128:(ofc + 1) * 128, :],
                                  sgt[b][:, :], s_st)
            sg_done = s_st.v

            for tc in range(NTC):
                W(SY, s_out)
                W(SC, s_out)
                dma(SY, xb2[:, :], x_own[tc * 128:(tc + 1) * 128, :], s_al)
                W(SY, s_st, sg_done)
                for c in range(32):
                    dma(SY, sgt2[:, c * 128:(c + 1) * 128],
                        sg_sc[c * 128:(c + 1) * 128, tc * 128:(tc + 1) * 128],
                        s_tl, transpose=True)
                W(VE, s_al)
                W(VE, s_tl)
                iv = VE.tensor_tensor(xb2[:, :], xb2[:, :], sgt2[:, :], OP.add)
                s_dve.inc(iv)
                W(SY, s_dve)
                dma(SY, y[tc * 128:(tc + 1) * 128, :], xb2[:, :], s_out)
            if DEBUG:
                barrier()
                dma(GP, dbg["d_xn"], xn_own_sc, s_out)
                dma(GP, dbg["d_qkv"], qkv_sc[:, S_PRE:S_PRE + T_OWN], s_out)
                dma(GP, dbg["d_attn"], attn_sc, s_out)
                dma(GP, dbg["d_gate"], gate_sc, s_out)
                dma(GP, dbg["d_h"], h_sc, s_out)
                dma(GP, dbg["d_sg"], sg_sc, s_out)
                dma(GP, dbg["d_rstd"], rstd_sc, s_out)
                W(GP, s_out)
            W(SY, s_out)
        st56.close()
    return nc


def kernel(**inputs):
    per_core = _prep(inputs)
    nc = build_nc()
    res = run_bass_kernel_spmd(nc, per_core, core_ids=list(range(8)))
    global LAST_RESULTS
    LAST_RESULTS = res
    y = np.zeros((B, SEQ, HID), dtype=np.float32)
    for c in range(8):
        s, p = c // 4, c % 4
        y[s, p * T_OWN:(p + 1) * T_OWN] = res.results[c]["y"]
    return y


# version 06 (final)
# speedup vs baseline: 1.1338x; 1.1338x over previous
import contextlib
import numpy as np
import concourse.bass as bass
import concourse.mybir as mybir
from concourse.bass_utils import run_bass_kernel_spmd

F32 = mybir.dt.float32
F32R = mybir.dt.float32r
BF16 = mybir.dt.bfloat16
I32 = mybir.dt.int32
AF = mybir.ActivationFunctionType
OP = mybir.AluOpType

B, SEQ, HID = 2, 4096, 4096
HEADS, HD, BLK = 32, 128, 256
NB = SEQ // BLK
INTER = 11008
NGK = HID // 128
NGI = INTER // 128
EPS = 1e-5

T_OWN = 1024
DEBUG = False
LAST_RESULTS = None
S_PRE = 3072
NB_PRE = S_PRE // BLK
NTH = T_OWN // 512
ISL = 256
NSL = INTER // ISL
NTC = T_OWN // 128


def _simm(v):
    v &= 0xFFFFFFFF
    return v - (1 << 32) if v >= (1 << 31) else v


def _decay_tables():
    f32 = np.float32
    base = f32(1.0) / f32(2.0) ** f32(8.0 / HEADS)
    exponent = np.arange(HEADS, dtype=np.float32) + f32(1.0)
    factor = f32(1.0 - 1 / (32 - 1 + 1e-5) + 1e-5)
    slope = (base ** exponent * factor).astype(np.float32)
    r = np.arange(BLK, dtype=np.float32) + f32(1.0)
    q_decay = np.exp(-slope[:, None] * r[None, :]).astype(np.float32)
    k_decay = np.exp(-slope[:, None] * (f32(BLK) - r)[None, :]).astype(np.float32)
    diff = r[:, None] - r[None, :]
    diag = slope[:, None, None] * diff[None, :, :]
    diag = np.where(diag >= 0, -diag, -np.inf).astype(np.float32)
    diag_decay = np.exp(diag).astype(np.float32)
    block_decay = np.exp(-slope * f32(BLK)).astype(np.float32)
    return slope, q_decay, k_decay, diag_decay, block_decay


def _tile_qw_gu(qw):
    q = qw.reshape(NGK, 16, INTER // 128, 128)
    return np.ascontiguousarray(q.transpose(1, 2, 0, 3).reshape(16, INTER // 128, NGK * 128))


def _tile_qw_d(qw):
    q = qw.reshape(NGI, 16, 8, 512)
    return np.ascontiguousarray(q.transpose(1, 2, 0, 3))


def _tile_w(wt):
    K, OF = wt.shape
    t = wt.reshape(K // 128, 128, OF // 128, 128)
    return np.ascontiguousarray(t.transpose(2, 1, 0, 3))


def _prep(inputs):
    for k, n in (("L0_ffn_gate_g_idx", HID), ("L0_ffn_up_g_idx", HID),
                 ("L0_ffn_down_g_idx", INTER)):
        gi = np.asarray(inputs[k])
        assert np.array_equal(gi, np.arange(n) // 128), f"unexpected {k}"
    x = np.ascontiguousarray(np.asarray(inputs["L0_attn_hidden_states"], dtype=np.float32))
    qkv_w = np.asarray(inputs["L0_attn_qkv_w"], dtype=np.float32)
    out_w = np.asarray(inputs["L0_attn_out_w"], dtype=np.float32)
    gate_w = np.asarray(inputs["L0_attn_gate_w"], dtype=np.float32)
    norm_w = np.asarray(inputs["L0_attn_norm_w"], dtype=np.float32)

    shared = {
        "wqkv": _tile_w(np.ascontiguousarray(qkv_w.T)),
        "wgate": _tile_w(np.ascontiguousarray(gate_w.T)),
        "wout": _tile_w(np.ascontiguousarray(out_w.T)),
        "normc": np.ascontiguousarray(norm_w.reshape(32, 128).T),
        "g_qw": _tile_qw_gu(np.asarray(inputs["L0_ffn_gate_qweight"], dtype=np.int32)),
        "g_s": np.ascontiguousarray(np.asarray(inputs["L0_ffn_gate_scales"], dtype=np.float32)),
        "g_qz": np.ascontiguousarray(np.asarray(inputs["L0_ffn_gate_qzeros"], dtype=np.int32)),
        "u_qw": _tile_qw_gu(np.asarray(inputs["L0_ffn_up_qweight"], dtype=np.int32)),
        "u_s": np.ascontiguousarray(np.asarray(inputs["L0_ffn_up_scales"], dtype=np.float32)),
        "u_qz": np.ascontiguousarray(np.asarray(inputs["L0_ffn_up_qzeros"], dtype=np.int32)),
        "d_qw": _tile_qw_d(np.asarray(inputs["L0_ffn_down_qweight"], dtype=np.int32)),
        "d_s": np.ascontiguousarray(np.asarray(inputs["L0_ffn_down_scales"], dtype=np.float32)),
        "d_qz": np.ascontiguousarray(np.asarray(inputs["L0_ffn_down_qzeros"], dtype=np.int32)),
    }

    _, q_decay, k_decay, diag_decay, _ = _decay_tables()
    shared["decayT"] = np.ascontiguousarray(diag_decay.transpose(0, 2, 1))
    qd = q_decay.reshape(HEADS, 2, 128).transpose(2, 0, 1).reshape(128, HEADS * 2)
    kd = k_decay.reshape(HEADS, 2, 128).transpose(2, 0, 1).reshape(128, HEADS * 2)
    shared["qdec"] = np.ascontiguousarray(qd)
    shared["kdec"] = np.ascontiguousarray(kd)
    j = np.arange(128) % 8
    shared["maskv"] = ((np.int64(15) << (4 * j)).astype(np.int64)
                       .astype(np.uint32).view(np.int32).reshape(128, 1).copy())
    shared["pow2v"] = (2.0 ** (-4.0 * j)).astype(np.float32).reshape(128, 1).copy()
    shared["pow2c"] = (2.0 ** (-4.0 * j)).astype(np.float32).reshape(1, 128).copy()
    shared["eye"] = np.eye(128, dtype=np.float32)
    ind = np.zeros((128, 2 * NGI - 1), dtype=np.float32)
    ind[:, NGI - 1] = 1.0
    shared["ind"] = ind

    per_core = []
    for c in range(8):
        s, p = c // 4, c % 4
        S = p * T_OWN
        x_seq = x[s]
        m = dict(shared)
        m["x_own"] = np.ascontiguousarray(x_seq[S:S + T_OWN])
        x_pre = np.zeros((S_PRE, HID), dtype=np.float32)
        if S > 0:
            x_pre[S_PRE - S:] = x_seq[:S]
        m["x_pre"] = x_pre
        per_core.append(m)
    return per_core


class Trk:
    def __init__(self, stack, nc, name):
        self.h = stack.enter_context(nc.semaphore(name))
        self.v = 0
        self.name = name

    def inc(self, instr, n=1):
        instr.then_inc(self.h, n)
        self.v += n
        return self.v


def build_nc():
    _, _, _, _, bdec = _decay_tables()
    nc = bass.Bass("TRN2", target_bir_lowering=False, debug=False)
    dt = lambda n, s, d, k="ExternalInput": nc.dram_tensor(n, s, d, kind=k).ap()

    x_own = dt("x_own", [T_OWN, HID], F32)
    x_pre = dt("x_pre", [S_PRE, HID], F32)
    wqkv = dt("wqkv", [96, 128, 32, 128], F32)
    wgate = dt("wgate", [32, 128, 32, 128], F32)
    wout = dt("wout", [32, 128, 32, 128], F32)
    normc = dt("normc", [128, 32], F32)
    g_qw = dt("g_qw", [16, INTER // 128, NGK * 128], I32)
    g_s = dt("g_s", [NGK, INTER], F32)
    g_qz = dt("g_qz", [NGK, INTER // 8], I32)
    u_qw = dt("u_qw", [16, INTER // 128, NGK * 128], I32)
    u_s = dt("u_s", [NGK, INTER], F32)
    u_qz = dt("u_qz", [NGK, INTER // 8], I32)
    d_qw = dt("d_qw", [16, 8, NGI, 512], I32)
    d_s = dt("d_s", [NGI, HID], F32)
    d_qz = dt("d_qz", [NGI, HID // 8], I32)
    decayT = dt("decayT", [HEADS, BLK, BLK], F32)
    qdec = dt("qdec", [128, HEADS * 2], F32)
    kdec = dt("kdec", [128, HEADS * 2], F32)
    maskv = dt("maskv", [128, 1], I32)
    pow2v = dt("pow2v", [128, 1], F32)
    pow2c = dt("pow2c", [1, 128], F32)
    eye = dt("eye", [128, 128], F32)
    ind = dt("ind", [128, 2 * NGI - 1], F32)
    y = dt("y", [T_OWN, HID], F32, "ExternalOutput")
    dbg = {}
    if DEBUG:
        dbg["d_xn"] = dt("d_xn", [T_OWN, HID], BF16, "ExternalOutput")
        dbg["d_qkv"] = dt("d_qkv", [96 * 128, T_OWN], BF16, "ExternalOutput")
        dbg["d_attn"] = dt("d_attn", [T_OWN, HID], BF16, "ExternalOutput")
        dbg["d_gate"] = dt("d_gate", [HID, T_OWN], BF16, "ExternalOutput")
        dbg["d_h"] = dt("d_h", [INTER, T_OWN], BF16, "ExternalOutput")
        dbg["d_sg"] = dt("d_sg", [HID, T_OWN], BF16, "ExternalOutput")
        dbg["d_rstd"] = dt("d_rstd", [T_OWN, 1], F32, "ExternalOutput")

    si = lambda n, s, d: nc.dram_tensor(n, s, d, kind="Internal").ap()
    xn_own_sc = si("xn_own_sc", [T_OWN, HID], BF16)
    xn_pre_sc = si("xn_pre_sc", [S_PRE, HID], BF16)
    qkv_sc = si("qkv_sc", [96 * 128, SEQ], BF16)
    attn_sc = si("attn_sc", [T_OWN, HID], BF16)
    gate_sc = si("gate_sc", [HID, T_OWN], BF16)
    h_sc = si("h_sc", [INTER, T_OWN], BF16)
    sg_sc = si("sg_sc", [HID, T_OWN], BF16)
    rstd_sc = si("rstd_sc", [T_OWN, 1], F32)
    gsb_sc = si("gsb_sc", [NGK, INTER], BF16)
    usb_sc = si("usb_sc", [NGK, INTER], BF16)
    dsb_sc = si("dsb_sc", [NGI, HID], BF16)
    szg_sc = si("szg_sc", [NGK, INTER], BF16)
    szu_sc = si("szu_sc", [NGK, INTER], BF16)

    stack = contextlib.ExitStack()
    with stack:
        sb = lambda n, s, d, **kw: stack.enter_context(nc.sbuf_tensor(n, s, d, **kw))
        psb = [stack.enter_context(nc.psum_tensor(f"ps{i}", [128, 512], F32))
               for i in range(8)]

        t_qdec = sb("t_qdec", [128, HEADS * 2], F32)
        t_kdec = sb("t_kdec", [128, HEADS * 2], F32)
        t_mask = sb("t_mask", [128, 1], I32)
        t_pow2 = sb("t_pow2", [128, 1], F32)
        t_p2cr = sb("t_p2cr", [1, 128], F32R)
        t_eye = sb("t_eye", [128, 128], BF16)
        t_one = sb("t_one", [1, 128], BF16)
        t_norm = sb("t_norm", [128, 32], F32)
        t_ind = sb("t_ind", [128, 2 * NGI - 1], BF16)
        t_szd = sb("t_szd", [NGI, HID], BF16)
        t_const = sb("t_const", [128, 2], F32)

        s_wl = Trk(stack, nc, "s_wl")
        s_al = Trk(stack, nc, "s_al")
        s_tl = Trk(stack, nc, "s_tl")
        s_st = Trk(stack, nc, "s_st")
        s_pe = Trk(stack, nc, "s_pe")
        s_act = Trk(stack, nc, "s_act")
        s_dve = Trk(stack, nc, "s_dve")
        s_pl = Trk(stack, nc, "s_pl")
        s_out = Trk(stack, nc, "s_out")
        ALL_TRK = [s_wl, s_al, s_tl, s_st, s_pe, s_act, s_dve, s_pl, s_out]

        wait_cache = {}

        def W(eng, trk, v=None):
            v = trk.v if v is None else v
            if v <= 0:
                return
            key = (id(eng), trk.name)
            if wait_cache.get(key, 0) >= v:
                return
            eng.wait_ge(trk.h, v)
            wait_cache[key] = v

        def dma(eng, out, in_, trk, **kw):
            eng.dma_start(out=out, in_=in_, **kw).then_inc(trk.h, 16)
            trk.v += 16
            return trk.v

        SY, SC, VE, PE, GP = nc.sync, nc.scalar, nc.vector, nc.tensor, nc.gpsimd
        ENGINES = [SY, SC, VE, PE, GP]

        def barrier():
            for e in ENGINES:
                for t in ALL_TRK:
                    W(e, t)

        dma(SY, t_qdec[:, :], qdec, s_al)
        dma(SY, t_kdec[:, :], kdec, s_al)
        dma(SY, t_mask[:, :], maskv, s_al)
        dma(SY, t_pow2[:, :], pow2v, s_al)
        dma(SY, t_p2cr[:, :], pow2c.bitcast(F32R), s_al)
        dma(GP, t_eye[:, :], eye, s_wl)
        dma(SY, t_norm[:, :], normc, s_al)
        dma(GP, t_ind[:, :], ind, s_wl)
        scale_mark = s_wl.v
        s_dve.inc(VE.memset(t_one[:, :], 1.0))
        s_dve.inc(VE.memset(t_const[:, 0:1], EPS))
        s_dve.inc(VE.memset(t_const[:, 1:2], 1.0 / HID))

        with contextlib.ExitStack() as pre:
            sbl = lambda n, s, d: pre.enter_context(nc.sbuf_tensor(n, s, d))
            t_zi = sbl("t_zi", [NGI, INTER], I32)
            t_zf = sbl("t_zf", [NGI, INTER], F32)
            t_qzb = sbl("t_qzb", [NGI, INTER // 8], I32)
            t_sf = sbl("t_sf", [NGI, INTER], F32)
            t_szb = sbl("t_szb", [NGI, INTER], BF16)
            for (qz_d, s_d, spill_to, ng, width) in (
                (g_qz, g_s, szg_sc, NGK, INTER),
                (u_qz, u_s, szu_sc, NGK, INTER),
                (d_qz, d_s, None, NGI, HID),
            ):
                w8 = width // 8
                W(SY, s_dve)
                W(SY, s_st)
                dma(SY, t_qzb[0:ng, 0:w8], qz_d, s_al)
                dma(SY, t_sf[0:ng, 0:width], s_d, s_al)
                W(VE, s_al)
                for jn in range(8):
                    VE.tensor_scalar(t_zi[0:ng, jn:width:8], t_qzb[0:ng, 0:w8],
                                     _simm(15 << (4 * jn)), None, OP.bitwise_and)
                for jn in range(8):
                    VE.tensor_scalar(t_zf[0:ng, jn:width:8], t_zi[0:ng, jn:width:8],
                                     float(2.0 ** (-4 * jn)), None, OP.mult)
                VE.tensor_scalar(t_zf[0:ng, 0:width], t_zf[0:ng, 0:width],
                                 1.0, None, OP.add)
                tgt = t_szd if spill_to is None else t_szb
                iv = VE.scalar_tensor_tensor(tgt[0:ng, 0:width], t_zf[0:ng, 0:width],
                                             -1.0, t_sf[0:ng, 0:width],
                                             OP.mult, OP.mult)
                s_dve.inc(iv)
                if spill_to is not None:
                    W(SY, s_dve)
                    dma(SY, spill_to, t_szb[0:ng, 0:width], s_st)
            barrier()
        sz_spill_mark = s_st.v

        NTILE = NTC + S_PRE // 128
        with contextlib.ExitStack() as p0:
            sbl = lambda n, s, d, **k: p0.enter_context(nc.sbuf_tensor(n, s, d, **k))
            xb = [sbl(f"p0x{i}", [128, HID], F32) for i in range(2)]
            dummy = sbl("p0dummy", [128, HID], F32)
            xnb = [sbl(f"p0n{i}", [128, HID], BF16) for i in range(2)]
            st_sum = sbl("p0sum", [128, NTILE], F32)
            st_ss = sbl("p0ss", [128, NTILE], F32)
            st_t = sbl("p0t", [128, NTILE], F32)
            tiles = [(x_own[t * 128:(t + 1) * 128, :],
                      xn_own_sc[t * 128:(t + 1) * 128, :]) for t in range(NTC)]
            tiles += [(x_pre[t * 128:(t + 1) * 128, :],
                       xn_pre_sc[t * 128:(t + 1) * 128, :]) for t in range(S_PRE // 128)]
            act_war = [0, 0]
            for i, (src, dst) in enumerate(tiles):
                b = i % 2
                W(SY, s_act, act_war[b])
                dma(SY, xb[b][:, :], src, s_al)
                W(SC, s_al)
                SC.activation(dummy[:, :], xb[b][:, :], AF.Copy,
                              accum_out=st_sum[:, i:i + 1])
                i1 = SC.activation(dummy[:, :], xb[b][:, :], AF.Square,
                                   accum_out=st_ss[:, i:i + 1])
                act_war[b] = s_act.inc(i1)
            W(VE, s_act)
            iv = VE.tensor_scalar(st_sum[:, :], st_sum[:, :], -1.0 / HID, None,
                                  OP.mult)
            s_dve.inc(iv)
            W(VE, s_dve)
            iv = VE.tensor_tensor(st_t[:, :], st_sum[:, :], st_sum[:, :], OP.mult)
            s_dve.inc(iv)
            W(VE, s_dve)
            iv = VE.scalar_tensor_tensor(st_ss[:, :], st_ss[:, :], 1.0 / HID,
                                         st_t[:, :], OP.mult, OP.subtract)
            s_dve.inc(iv)
            W(SC, s_dve)
            i3 = SC.activation(st_ss[:, :], st_ss[:, :], AF.Sqrt,
                               bias=t_const[:, 0:1])
            s_act.inc(i3)
            W(VE, s_act)
            iv = VE.reciprocal(st_ss[:, :], st_ss[:, :])
            s_dve.inc(iv)
            W(VE, s_dve)
            nrm_war = [0, 0]
            sp_war = [0, 0]
            for i, (src, dst) in enumerate(tiles):
                b = i % 2
                W(SY, s_dve, nrm_war[b])
                dma(SY, xb[b][:, :], src, s_al)
                W(VE, s_al)
                W(VE, s_st, sp_war[b])
                iv = VE.tensor_scalar(xnb[b][:, :], xb[b][:, :],
                                      st_sum[:, i:i + 1], st_ss[:, i:i + 1],
                                      OP.add, OP.mult)
                nrm_war[b] = s_dve.inc(iv)
                W(SY, s_dve)
                sp_war[b] = dma(SY, dst, xnb[b][:, :], s_st)
            barrier()
        p0_spill_done = s_st.v

        kv_chunks = [c for c in range(96) if c % 3 != 0]
        FH_T = S_PRE // 2
        FH_TILES = FH_T // 512

        pstate = {"war_pe": [0, 0], "war_act": [0, 0], "war_sp": [0, 0]}

        def proj_pass(wb, evb, xn_tile_ap, ntile, chunk_list, wdram, act_fn,
                      spill_fn, bank_groups):
            for idx, ofc in enumerate(chunk_list):
                bwt = idx % 2
                W(GP, s_pe, pstate["war_pe"][bwt])
                dma(GP, wb[bwt][:, :], wdram[ofc], s_wl)
                g = idx % 2
                banks = bank_groups[g]
                W(PE, s_wl)
                W(PE, s_act, pstate["war_act"][g])
                mm = None
                for c in range(32):
                    for t in range(ntile):
                        mm = PE.matmul(psb[banks[t]][:, :],
                                       wb[bwt][:, c * 128:(c + 1) * 128],
                                       xn_tile_ap(c, t),
                                       start=(c == 0), stop=(c == 31))
                pstate["war_pe"][bwt] = s_pe.inc(mm)
                W(SC, s_pe)
                W(SC, s_st, pstate["war_sp"][g])
                ia = None
                for t in range(ntile):
                    ia = SC.activation(evb[g][:, t * 512:(t + 1) * 512],
                                       psb[banks[t]][:, :], act_fn)
                pstate["war_act"][g] = s_act.inc(ia)
                W(SY, s_act)
                pstate["war_sp"][g] = dma(SY, spill_fn(ofc, ntile),
                                          evb[g][:, 0:ntile * 512], s_st)

        st14 = contextlib.ExitStack()
        with contextlib.ExitStack() as p1:
            sbl = lambda n, s, d, **k: p1.enter_context(nc.sbuf_tensor(n, s, d, **k))
            wb = [sbl(f"p1w{i}", [128, 4096], BF16) for i in range(2)]
            evb = [sbl(f"p1e{i}", [128, 3 * 512], BF16) for i in range(2)]

            ttile = [p1.enter_context(nc.sbuf_tensor(f"p1t{i}", [128, HID], BF16))
                     for i in range(2)]
            psbf67 = [psb[6].bitcast(BF16), psb[7].bitcast(BF16)]
            tr_state = {"tl_war": [0, 0], "bank_war": [0, 0]}

            def build_fm(dst_ap_fn, src_sc, row0, ntc):
                W(PE, s_wl)
                for tc in range(ntc):
                    tb = tc % 2
                    W(SY, s_pe, tr_state["tl_war"][tb])
                    dma(SY, ttile[tb][:, :],
                        src_sc[row0 + tc * 128:row0 + (tc + 1) * 128, :], s_al)
                    W(PE, s_al)
                    for cg in range(8):
                        bk = cg % 2
                        W(PE, s_dve, tr_state["bank_war"][bk])
                        it = None
                        for ci in range(4):
                            c = cg * 4 + ci
                            it = PE.transpose(psbf67[bk][:, ci * 128:(ci + 1) * 128],
                                              ttile[tb][:, c * 128:(c + 1) * 128],
                                              t_eye[:, :])
                        mtr = s_pe.inc(it)
                        W(VE, s_pe, mtr)
                        iv = None
                        for ci in range(4):
                            c = cg * 4 + ci
                            iv = VE.tensor_copy(dst_ap_fn(c, tc),
                                                psbf67[bk][:, ci * 128:(ci + 1) * 128])
                        tr_state["bank_war"][bk] = s_dve.inc(iv)
                    tr_state["tl_war"][tb] = s_pe.v
                W(PE, s_dve)

            with contextlib.ExitStack() as p1f:
                xnf = p1f.enter_context(
                    nc.sbuf_tensor("p1xnf", [128, 32 * FH_T], BF16,
                                   is_transpose_dest=True))
                for fh in range(2):
                    W(SY, s_st, p0_spill_done)
                    if fh > 0:
                        W(SY, s_pe)
                    build_fm(lambda c, tc: xnf[:, c * FH_T + tc * 128:c * FH_T + (tc + 1) * 128],
                             xn_pre_sc, fh * FH_T, FH_T // 128)
                    col0 = fh * FH_T

                    def spill_kv(ofc, ntile, col0=col0):
                        return qkv_sc[ofc * 128:(ofc + 1) * 128,
                                      col0:col0 + ntile * 512]

                    proj_pass(wb, evb,
                              lambda c, t: xnf[:, c * FH_T + t * 512:c * FH_T + (t + 1) * 512],
                              FH_TILES, kv_chunks, wqkv, AF.Silu, spill_kv,
                              [(0, 1, 2), (3, 4, 5)])

            xno = st14.enter_context(
                nc.sbuf_tensor("xno", [128, 32 * T_OWN], BF16,
                               is_transpose_dest=True, side="right"))
            W(SY, s_st, p0_spill_done)
            W(SY, s_pe)
            build_fm(lambda c, tc: xno[:, c * T_OWN + tc * 128:c * T_OWN + (tc + 1) * 128],
                     xn_own_sc, 0, NTC)

            def spill_own(ofc, ntile):
                return qkv_sc[ofc * 128:(ofc + 1) * 128, S_PRE:S_PRE + ntile * 512]

            proj_pass(wb, evb,
                      lambda c, t: xno[:, c * T_OWN + t * 512:c * T_OWN + (t + 1) * 512],
                      NTH, list(range(96)), wqkv, AF.Silu, spill_own,
                      [(0, 1), (2, 3)])
        p1_done_st = s_st.v
        barrier()

        with contextlib.ExitStack() as p2:
            sbl = lambda n, s, d, **k: p2.enter_context(nc.sbuf_tensor(n, s, d, **k))
            sdt = sbl("sdt", [128, HEADS * 2 * 256], F32)
            lkv = sbl("lkv", [128, HEADS * 128], F32)
            lkv16 = sbl("lkv16", [128, HEADS * 128], BF16)
            mgrp = [sbl(f"mgrp{i}", [128, 12 * 256], BF16) for i in range(2)]
            mvt = [sbl(f"mvt{i}", [128, 256], BF16) for i in range(2)]
            khat = [sbl(f"khat{i}", [128, 256], BF16) for i in range(2)]
            scd = [sbl(f"scd{i}", [128, 512], BF16) for i in range(2)]
            t1b = [sbl(f"t1b{i}", [128, 256], F32) for i in range(2)]
            attn_blk = sbl("attn_blk", [128, 2 * HID], BF16)

            for h in range(HEADS):
                for half in range(2):
                    dma(SY, sdt[:, (h * 2 + half) * 256:(h * 2 + half + 1) * 256],
                        decayT[h, half * 128:(half + 1) * 128, :], s_al)
            VE.memset(lkv[:, :], 0.0)
            s_dve.inc(VE.memset(lkv16[:, :], 0.0))

            unit_dve = {}
            pe_done = {}
            dve_done = {}
            grp_pe = [0, 0]
            attn_war = 0
            for n in range(NB):
                owned = n >= NB_PRE
                for hg in range(8):
                    gidx = n * 8 + hg
                    gu = gidx % 2
                    W(SY, s_pe, grp_pe[gu])
                    W(SY, s_st, p1_done_st)
                    dma(SY, mgrp[gu][:, :],
                        qkv_sc[hg * 12 * 128:(hg + 1) * 12 * 128,
                               n * 256:(n + 1) * 256]
                        .rearrange("(c p) w -> p c w", p=128), s_al)
                    grp_mark = s_al.v
                    for hh in range(4):
                        h = hg * 4 + hh
                        uidx = n * HEADS + h
                        u = uidx % 2
                        prev_pe = pe_done.get(uidx - 2, 0)
                        prev_dve = dve_done.get(uidx - 2, 0)
                        q_fm = mgrp[gu][:, (hh * 3) * 256:(hh * 3 + 1) * 256]
                        k_fm = mgrp[gu][:, (hh * 3 + 1) * 256:(hh * 3 + 2) * 256]
                        v_fm = mgrp[gu][:, (hh * 3 + 2) * 256:(hh * 3 + 3) * 256]
                        bs, bo, bi, bu = (0, 1, 2, 3) if u == 0 else (4, 5, 6, 7)
                        psT = psb[bs].bitcast(BF16)
                        W(PE, s_al, grp_mark)
                        W(PE, s_dve, prev_dve)
                        it = None
                        for half in range(2):
                            it = PE.transpose(psT[:, half * 128:(half + 1) * 128],
                                              k_fm[:, half * 128:(half + 1) * 128],
                                              t_eye[:, :])
                            it = PE.transpose(psT[:, (2 + half) * 128:(3 + half) * 128],
                                              v_fm[:, half * 128:(half + 1) * 128],
                                              t_eye[:, :])
                        tmark = s_pe.inc(it)
                        W(VE, s_pe, tmark)
                        W(VE, s_pe, prev_pe)
                        iv = None
                        for half in range(2):
                            VE.tensor_scalar(
                                khat[u][:, half * 128:(half + 1) * 128],
                                psT[:, half * 128:(half + 1) * 128],
                                t_kdec[:, h * 2 + half:h * 2 + half + 1], None,
                                OP.mult)
                            iv = VE.tensor_copy(
                                mvt[u][:, half * 128:(half + 1) * 128],
                                psT[:, (2 + half) * 128:(3 + half) * 128])
                        evac_mark = s_dve.inc(iv)
                        if owned:
                            W(PE, s_dve, evac_mark)
                            ip = None
                            for half in range(2):
                                ip = PE.matmul(psb[bs][:, half * 256:(half + 1) * 256],
                                               k_fm[:, half * 128:(half + 1) * 128],
                                               q_fm, start=True, stop=True)
                            sc_mark = s_pe.inc(ip)
                            W(VE, s_pe, sc_mark)
                            ivv = None
                            for half in range(2):
                                ivv = VE.tensor_tensor(
                                    scd[u][:, half * 256:(half + 1) * 256],
                                    psb[bs][:, half * 256:(half + 1) * 256],
                                    sdt[:, (h * 2 + half) * 256:(h * 2 + half + 1) * 256],
                                    OP.mult)
                            scd_mark = s_dve.inc(ivv)
                            W(PE, s_dve, scd_mark)
                            W(PE, s_dve, unit_dve.get((n - 1, h), 0))
                            for tqh in range(2):
                                PE.matmul(psb[bo][:, tqh * 128:(tqh + 1) * 128],
                                          scd[u][:, tqh * 128:tqh * 128 + 128],
                                          mvt[u][:, 0:128], start=True, stop=False)
                                PE.matmul(psb[bo][:, tqh * 128:(tqh + 1) * 128],
                                          scd[u][:, 256 + tqh * 128:256 + tqh * 128 + 128],
                                          mvt[u][:, 128:256], start=False, stop=True)
                                PE.matmul(psb[bi][:, tqh * 128:(tqh + 1) * 128],
                                          q_fm[:, tqh * 128:(tqh + 1) * 128],
                                          lkv16[:, h * 128:(h + 1) * 128],
                                          start=True, stop=True)
                        else:
                            W(PE, s_dve, evac_mark)
                        W(PE, s_dve, prev_dve)
                        PE.matmul(psb[bu][:, 0:128], khat[u][:, 0:128],
                                  mvt[u][:, 0:128], start=True, stop=False)
                        ip2 = PE.matmul(psb[bu][:, 0:128], khat[u][:, 128:256],
                                        mvt[u][:, 128:256], start=False, stop=True)
                        pe_done[uidx] = s_pe.inc(ip2)
                        W(VE, s_pe, pe_done[uidx])
                        if owned:
                            W(VE, s_st, attn_war)
                            it1 = None
                            for tqh in range(2):
                                it1 = VE.tensor_scalar(
                                    t1b[u][:, tqh * 128:(tqh + 1) * 128],
                                    psb[bi][:, tqh * 128:(tqh + 1) * 128],
                                    t_qdec[:, h * 2 + tqh:h * 2 + tqh + 1], None,
                                    OP.mult)
                            s_dve.inc(it1)
                            W(VE, s_dve)
                            for tqh in range(2):
                                VE.tensor_tensor(
                                    attn_blk[:, tqh * HID + h * 128:tqh * HID + (h + 1) * 128],
                                    psb[bo][:, tqh * 128:(tqh + 1) * 128],
                                    t1b[u][:, tqh * 128:(tqh + 1) * 128], OP.add)
                        ist = VE.scalar_tensor_tensor(lkv[:, h * 128:(h + 1) * 128],
                                                      lkv[:, h * 128:(h + 1) * 128],
                                                      float(bdec[h]),
                                                      psb[bu][:, 0:128],
                                                      OP.mult, OP.add)
                        s_dve.inc(ist)
                        W(VE, s_dve)
                        iu = VE.tensor_copy(lkv16[:, h * 128:(h + 1) * 128],
                                            lkv[:, h * 128:(h + 1) * 128])
                        dve_done[uidx] = s_dve.inc(iu)
                        unit_dve[(n, h)] = dve_done[uidx]
                    grp_pe[gu] = s_pe.v
                if owned:
                    W(SY, s_dve)
                    r0 = (n - NB_PRE) * 256
                    for tqh in range(2):
                        attn_war = dma(
                            SY, attn_sc[r0 + tqh * 128:r0 + (tqh + 1) * 128, :],
                            attn_blk[:, tqh * HID:(tqh + 1) * HID], s_st)
        barrier()

        with contextlib.ExitStack() as p3:
            sbl = lambda n, s, d, **k: p3.enter_context(nc.sbuf_tensor(n, s, d, **k))
            wb = [sbl(f"p3w{i}", [128, 4096], BF16) for i in range(2)]
            evb = [sbl(f"p3e{i}", [128, 2 * 512], BF16) for i in range(2)]
            pstate["war_pe"] = [0, 0]
            pstate["war_act"] = [0, 0]
            pstate["war_sp"] = [0, 0]

            def spill_gate(ofc, ntile):
                return gate_sc[ofc * 128:(ofc + 1) * 128, 0:ntile * 512]

            proj_pass(wb, evb,
                      lambda c, t: xno[:, c * T_OWN + t * 512:c * T_OWN + (t + 1) * 512],
                      NTH, list(range(32)), wgate, AF.Sigmoid, spill_gate,
                      [(0, 1), (2, 3)])
        barrier()

        P4SL = 128
        P4N = INTER // P4SL
        t_hg = sb("t_hg", [NGI, T_OWN], BF16)
        with contextlib.ExitStack() as p4:
            sbl = lambda n, s, d, **k: p4.enter_context(nc.sbuf_tensor(n, s, d, **k))
            t_xg = sbl("t_xg", [NGK, T_OWN], BF16)
            rep_all = {m: sbl(f"p4r{m}", [128, NGK * P4SL], I32) for m in "gu"}
            sbc_sh = sbl("p4b", [128, NGK * P4SL], F32)
            sbc_all = {"g": sbc_sh, "u": sbc_sh}
            srow = {m: sbl(f"p4s{m}", [1, NGK * P4SL], F32R) for m in "gu"}
            dq = {m: [sbl(f"p4d{m}{i}", [128, NGK * P4SL], BF16) for i in range(2)]
                  for m in "gu"}
            szb = {m: [sbl(f"p4z{m}{i}", [NGK, P4SL], BF16) for i in range(2)]
                   for m in "gu"}
            h_t = [sbl(f"p4h{i}", [128, T_OWN], BF16) for i in range(2)]
            sil = [sbl(f"p4sil{i}", [128, 512], BF16) for i in range(2)]

            for th in range(NTH):
                mm = None
                for c in range(32):
                    mm = PE.matmul(psb[6 + th][0:NGK, :],
                                   t_ind[:, (NGI - 1 - c):(NGI - 1 - c) + NGK],
                                   xno[:, c * T_OWN + th * 512:c * T_OWN + (th + 1) * 512],
                                   start=(c == 0), stop=(c == 31))
                s_pe.inc(mm)
            W(SC, s_pe)
            ia = None
            for th in range(NTH):
                ia = SC.activation(t_xg[0:NGK, th * 512:(th + 1) * 512],
                                   psb[6 + th][0:NGK, :], AF.Copy)
            xg_mark = s_act.inc(ia)
            W(PE, s_act, xg_mark)

            qw_d = {"g": g_qw, "u": u_qw}
            sc_d = {"g": g_s, "u": u_s}
            sz_sc = {"g": szg_sc, "u": szu_sc}

            stage_done = [0] * P4N
            sz_mark = [0] * P4N
            tt_done = {m: 0 for m in "gu"}
            bc_war = {4: 0, 5: 0}
            srow_pe = {m: 0 for m in "gu"}
            dq_war = [0, 0]
            szb_war = [0, 0]
            h_spill_war = [0, 0]
            hg_war = [0, 0]
            sil_war = [0, 0]
            ld_mark4 = [0] * P4N

            def stage_dma(j):
                sj = j % 2
                c0, c1 = j * P4SL, (j + 1) * P4SL
                for m in "gu":
                    W(SY, s_dve, tt_done[m])
                    dma(SY, rep_all[m][:, :],
                        qw_d[m][:, j, :].unsqueeze(1)
                        .broadcast_to([16, 8, NGK * P4SL]), s_al)
                    W(SY, s_pe, srow_pe[m])
                    dma(SY, srow[m][0:1, :],
                        sc_d[m][:, c0:c1].unsqueeze(0).bitcast(F32R), s_al)
                    W(SY, s_pe, szb_war[sj])
                    W(SY, s_st, sz_spill_mark)
                    dma(SY, szb[m][sj][0:NGK, :], sz_sc[m][:, c0:c1], s_al)
                ld_mark4[j] = s_al.v

            def stage_compute(j):
                sj = j % 2
                for m in "gu":
                    ibc = None
                    for part in range(8):
                        bank = 4 + (part % 2)
                        W(PE, s_al, ld_mark4[j])
                        W(PE, s_act, bc_war[bank])
                        ibc = PE.matmul(psb[bank][:, :], t_p2cr[0:1, :],
                                        srow[m][0:1, part * 512:(part + 1) * 512],
                                        start=True, stop=True)
                        bmk = s_pe.inc(ibc)
                        W(SC, s_pe, bmk)
                        W(SC, s_dve, tt_done["g"])
                        W(SC, s_dve, tt_done["u"])
                        ie = SC.activation(
                            sbc_sh[:, part * 512:(part + 1) * 512],
                            psb[bank][:, :], AF.Copy)
                        bc_war[bank] = s_act.inc(ie)
                    srow_pe[m] = s_pe.v
                    W(VE, s_al, ld_mark4[j])
                    W(VE, s_pe, dq_war[sj])
                    ig = VE.tensor_scalar(rep_all[m][:, :], rep_all[m][:, :],
                                          t_mask[:, 0:1], None, OP.bitwise_and)
                    s_dve.inc(ig)
                    W(VE, s_act, bc_war[4])
                    W(VE, s_act, bc_war[5])
                    im = VE.tensor_tensor(dq[m][sj][:, :], rep_all[m][:, :],
                                          sbc_sh[:, :], OP.mult)
                    tt_done[m] = s_dve.inc(im)
                stage_done[j] = s_dve.v
                sz_mark[j] = ld_mark4[j]

            def main_th(j, th):
                sj = j % 2
                gi = j
                W(PE, s_dve, stage_done[j])
                W(PE, s_al, sz_mark[j])
                for g in range(NGK):
                    PE.matmul(psb[th][:, :],
                              dq["g"][sj][:, g * P4SL:(g + 1) * P4SL],
                              xno[:, g * T_OWN + th * 512:g * T_OWN + (th + 1) * 512],
                              start=(g == 0), stop=False)
                    PE.matmul(psb[2 + th][:, :],
                              dq["u"][sj][:, g * P4SL:(g + 1) * P4SL],
                              xno[:, g * T_OWN + th * 512:g * T_OWN + (th + 1) * 512],
                              start=(g == 0), stop=False)
                W(PE, s_act, xg_mark)
                PE.matmul(psb[th][:, :], szb["g"][sj][0:NGK, :],
                          t_xg[0:NGK, th * 512:(th + 1) * 512],
                          start=False, stop=True)
                mmu = PE.matmul(psb[2 + th][:, :], szb["u"][sj][0:NGK, :],
                                t_xg[0:NGK, th * 512:(th + 1) * 512],
                                start=False, stop=True)
                gu_mark = s_pe.inc(mmu)
                W(SC, s_pe, gu_mark)
                W(SC, s_dve, sil_war[th])
                isl_ = SC.activation(sil[th][:, :], psb[th][:, :], AF.Silu)
                smark = s_act.inc(isl_)
                hb = gi % 2
                W(VE, s_act, smark)
                W(VE, s_st, h_spill_war[hb])
                W(VE, s_pe, hg_war[hb])
                ih = VE.tensor_tensor(h_t[hb][:, th * 512:(th + 1) * 512],
                                      sil[th][:, :], psb[2 + th][:, :], OP.mult)
                h_mark = s_dve.inc(ih)
                sil_war[th] = h_mark
                W(PE, s_dve, h_mark)
                ihg = PE.matmul(psb[6 + th][0:NGI, :],
                                t_ind[:, (NGI - 1 - gi):(NGI - 1 - gi) + NGI],
                                h_t[hb][:, th * 512:(th + 1) * 512],
                                start=(gi == 0), stop=(gi == P4N - 1))
                hg_war[hb] = s_pe.inc(ihg)

            stage_dma(0)
            stage_compute(0)
            for j in range(P4N):
                gi = j
                hb = gi % 2
                if j + 1 < P4N:
                    stage_dma(j + 1)
                main_th(j, 0)
                if j + 1 < P4N:
                    stage_compute(j + 1)
                main_th(j, 1)
                W(SY, s_dve)
                h_spill_war[hb] = dma(SY, h_sc[gi * 128:(gi + 1) * 128, :],
                                      h_t[hb][:, :], s_st)
                sj = j % 2
                dq_war[sj] = s_pe.v
                szb_war[sj] = s_pe.v
            W(SC, s_pe)
            ia = None
            for th in range(NTH):
                ia = SC.activation(t_hg[0:NGI, th * 512:(th + 1) * 512],
                                   psb[6 + th][0:NGI, :], AF.Copy)
            s_act.inc(ia)
        st14.close()
        barrier()

        st56 = contextlib.ExitStack()
        t_gfm = st56.enter_context(
            nc.sbuf_tensor("t_gfm", [128, 32 * T_OWN], BF16, side="right"))
        GB = 8
        gblocks = [(g0, min(GB, NGI - g0)) for g0 in range(0, NGI, GB)]
        with contextlib.ExitStack() as p5:
            sbl = lambda n, s, d, **k: p5.enter_context(nc.sbuf_tensor(n, s, d, **k))
            repd = [sbl(f"p5r{i}", [128, GB * 512], I32) for i in range(2)]
            sbcd = [sbl(f"p5s{i}", [128, GB * 512], F32) for i in range(2)]
            dqd = [sbl(f"p5w{i}", [128, GB * 512], BF16) for i in range(2)]
            hb = [sbl(f"p5h{i}", [128, GB * T_OWN], BF16) for i in range(2)]
            iters = [(og, g0, gn) for og in range(8) for (g0, gn) in gblocks]
            tt5_war = [0, 0]
            dq_war5 = [0, 0]
            h_war5 = [0, 0]
            ld5 = [0, 0]

            def p5_dma(idx):
                og, g0, gn = iters[idx]
                bb = idx % 2
                c0 = og * 512
                W(SY, s_dve, tt5_war[bb])
                dma(SY, repd[bb][:, 0:gn * 512],
                    d_qw[:, og, g0:g0 + gn, :].unsqueeze(1)
                    .broadcast_to([16, 8, gn, 512]), s_al)
                dma(SY, sbcd[bb][:, 0:gn * 512],
                    d_s[g0:g0 + gn, c0:c0 + 512].unsqueeze(0)
                    .broadcast_to([128, gn, 512]), s_al)
                W(SY, s_pe, h_war5[bb])
                dma(SY, hb[bb][:, 0:gn * T_OWN],
                    h_sc[g0 * 128:(g0 + gn) * 128, :]
                    .rearrange("(g p) t -> p g t", p=128), s_al)
                ld5[bb] = s_al.v

            p5_dma(0)
            for idx, (og, g0, gn) in enumerate(iters):
                bb = idx % 2
                c0 = og * 512
                if idx + 1 < len(iters):
                    p5_dma(idx + 1)
                W(VE, s_al, ld5[bb])
                W(VE, s_pe, dq_war5[bb])
                VE.tensor_scalar(repd[bb][:, 0:gn * 512], repd[bb][:, 0:gn * 512],
                                 t_mask[:, 0:1], None, OP.bitwise_and)
                im = VE.scalar_tensor_tensor(dqd[bb][:, 0:gn * 512],
                                             repd[bb][:, 0:gn * 512],
                                             t_pow2[:, 0:1],
                                             sbcd[bb][:, 0:gn * 512],
                                             OP.mult, OP.mult)
                tt5_war[bb] = s_dve.inc(im)
                W(PE, s_dve, tt5_war[bb])
                W(PE, s_al, ld5[bb])
                if og > 0 and g0 == 0:
                    W(PE, s_act)
                mm = None
                for g in range(gn):
                    for ofc in range(4):
                        for th in range(NTH):
                            mm = PE.matmul(
                                psb[ofc * 2 + th][:, :],
                                dqd[bb][:, g * 512 + ofc * 128:g * 512 + (ofc + 1) * 128],
                                hb[bb][:, g * T_OWN + th * 512:g * T_OWN + (th + 1) * 512],
                                start=(g0 + g == 0), stop=False)
                pe5 = s_pe.inc(mm)
                dq_war5[bb] = pe5
                h_war5[bb] = pe5
                if g0 + gn == NGI:
                    mm = None
                    for ofc in range(4):
                        for th in range(NTH):
                            mm = PE.matmul(psb[ofc * 2 + th][:, :],
                                           t_szd[0:NGI, c0 + ofc * 128:c0 + (ofc + 1) * 128],
                                           t_hg[0:NGI, th * 512:(th + 1) * 512],
                                           start=False, stop=True)
                    dn_mark = s_pe.inc(mm)
                    W(SC, s_pe, dn_mark)
                    ia = None
                    for ofc in range(4):
                        ofc_g = og * 4 + ofc
                        for th in range(NTH):
                            ia = SC.activation(
                                t_gfm[:, ofc_g * T_OWN + th * 512:ofc_g * T_OWN + (th + 1) * 512],
                                psb[ofc * 2 + th][:, :], AF.Silu)
                    s_act.inc(ia)
        barrier()

        with contextlib.ExitStack() as p6:
            sbl = lambda n, s, d, **k: p6.enter_context(nc.sbuf_tensor(n, s, d, **k))
            ab2 = [sbl("p6a0", [128, HID], BF16)] * 2
            ss2 = sbl("p6ss", [128, NTC], F32)
            rr = sbl("p6rr", [128, NTC], F32)
            rrow = sbl("p6row", [1, T_OWN], BF16)
            afm = [sbl(f"p6f{i}", [128, T_OWN], BF16, is_transpose_dest=True)
                   for i in range(2)]
            gfb = [sbl(f"p6g{i}", [128, T_OWN], BF16) for i in range(2)]
            t_prod = sbl("p6p", [128, 32 * T_OWN], BF16, side="right")
            wob = [sbl(f"p6w{i}", [128, 4096], BF16) for i in range(2)]
            sgt = [sbl(f"p6sg{i}", [128, T_OWN], BF16) for i in range(2)]
            xb2 = sbl("p6x", [128, HID], F32)
            sgt2 = sbl("p6s2", [128, HID], BF16, is_transpose_dest=True)
            dummy2 = xb2

            aw_s = 0
            for tc in range(NTC):
                W(SY, s_act, aw_s)
                dma(SY, ab2[0][:, :], attn_sc[tc * 128:(tc + 1) * 128, :], s_al)
                W(SC, s_al)
                isq = SC.activation(dummy2[:, :], ab2[0][:, :], AF.Square,
                                    accum_out=ss2[:, tc:tc + 1])
                aw_s = s_act.inc(isq)
            W(SC, s_act)
            i3 = SC.activation(ss2[:, :], ss2[:, :], AF.Sqrt,
                               bias=t_const[:, 0:1], scale=t_const[:, 1:2])
            s_act.inc(i3)
            W(VE, s_act)
            iv = VE.reciprocal(rr[:, :], ss2[:, :])
            s_dve.inc(iv)
            W(SY, s_dve)
            dma(SY, rstd_sc.rearrange("(i p) one -> p (i one)", p=128), rr[:, :],
                s_st, allow_slow_non_contiguous=True)
            W(GP, s_st)
            dma(GP, rrow[0:1, :], rstd_sc.rearrange("t one -> one t"), s_wl)
            W(PE, s_wl)
            PE.matmul(psb[0][:, :], t_one[0:1, :], rrow[0:1, 0:512],
                      start=True, stop=True)
            ib = PE.matmul(psb[1][:, :], t_one[0:1, :], rrow[0:1, 512:1024],
                           start=True, stop=True)
            rb_mark = s_pe.inc(ib)

            war_p6 = [0, 0]
            for fc in range(32):
                b = fc % 2
                W(SC, s_dve, war_p6[b])
                W(SY, s_dve, war_p6[b])
                for tc in range(NTC):
                    dma(SC, afm[b][:, tc * 128:(tc + 1) * 128],
                        attn_sc[tc * 128:(tc + 1) * 128, fc * 128:(fc + 1) * 128],
                        s_tl, transpose=True)
                dma(SY, gfb[b][:, :], gate_sc[fc * 128:(fc + 1) * 128, :], s_al)
                W(VE, s_tl)
                W(VE, s_al)
                VE.tensor_tensor(afm[b][:, :], afm[b][:, :], gfb[b][:, :], OP.mult)
                W(VE, s_pe, rb_mark)
                iv = None
                for th in range(NTH):
                    iv = VE.scalar_tensor_tensor(
                        t_prod[:, fc * T_OWN + th * 512:fc * T_OWN + (th + 1) * 512],
                        afm[b][:, th * 512:(th + 1) * 512],
                        t_norm[:, fc:fc + 1],
                        psb[th][:, :], OP.mult, OP.mult)
                war_p6[b] = s_dve.inc(iv)
            prod_mark = s_dve.v

            war_w6 = [0, 0]
            war_sg = [0, 0]
            sg_spill = [0, 0]
            for ofc in range(32):
                b = ofc % 2
                W(GP, s_pe, war_w6[b])
                dma(GP, wob[b][:, :], wout[ofc], s_wl)
                banks = (2, 3) if b == 0 else (4, 5)
                W(PE, s_wl)
                W(PE, s_dve, prod_mark)
                W(PE, s_dve, war_sg[b])
                mm = None
                for c in range(32):
                    for th in range(NTH):
                        mm = PE.matmul(psb[banks[th]][:, :],
                                       wob[b][:, c * 128:(c + 1) * 128],
                                       t_prod[:, c * T_OWN + th * 512:c * T_OWN + (th + 1) * 512],
                                       start=(c == 0), stop=(c == 31))
                war_w6[b] = s_pe.inc(mm)
                W(VE, s_pe, war_w6[b])
                W(VE, s_st, sg_spill[b])
                iv = None
                for th in range(NTH):
                    iv = VE.tensor_tensor(
                        sgt[b][:, th * 512:(th + 1) * 512],
                        psb[banks[th]][:, :],
                        t_gfm[:, ofc * T_OWN + th * 512:ofc * T_OWN + (th + 1) * 512],
                        OP.mult)
                war_sg[b] = s_dve.inc(iv)
                W(SY, s_dve, war_sg[b])
                sg_spill[b] = dma(SY, sg_sc[ofc * 128:(ofc + 1) * 128, :],
                                  sgt[b][:, :], s_st)
            sg_done = s_st.v

            for tc in range(NTC):
                W(SY, s_out)
                W(SC, s_out)
                dma(SY, xb2[:, :], x_own[tc * 128:(tc + 1) * 128, :], s_al)
                W(SY, s_st, sg_done)
                for c in range(32):
                    dma(SY, sgt2[:, c * 128:(c + 1) * 128],
                        sg_sc[c * 128:(c + 1) * 128, tc * 128:(tc + 1) * 128],
                        s_tl, transpose=True)
                W(VE, s_al)
                W(VE, s_tl)
                iv = VE.tensor_tensor(xb2[:, :], xb2[:, :], sgt2[:, :], OP.add)
                s_dve.inc(iv)
                W(SY, s_dve)
                dma(SY, y[tc * 128:(tc + 1) * 128, :], xb2[:, :], s_out)
            if DEBUG:
                barrier()
                dma(GP, dbg["d_xn"], xn_own_sc, s_out)
                dma(GP, dbg["d_qkv"], qkv_sc[:, S_PRE:S_PRE + T_OWN], s_out)
                dma(GP, dbg["d_attn"], attn_sc, s_out)
                dma(GP, dbg["d_gate"], gate_sc, s_out)
                dma(GP, dbg["d_h"], h_sc, s_out)
                dma(GP, dbg["d_sg"], sg_sc, s_out)
                dma(GP, dbg["d_rstd"], rstd_sc, s_out)
                W(GP, s_out)
            W(SY, s_out)
        st56.close()
    return nc


def kernel(**inputs):
    per_core = _prep(inputs)
    nc = build_nc()
    res = run_bass_kernel_spmd(nc, per_core, core_ids=list(range(8)))
    global LAST_RESULTS
    LAST_RESULTS = res
    y = np.zeros((B, SEQ, HID), dtype=np.float32)
    for c in range(8):
        s, p = c // 4, c % 4
        y[s, p * T_OWN:(p + 1) * T_OWN] = res.results[c]["y"]
    return y


# version 08
# speedup vs baseline: 1.1629x; 1.0256x over previous
import contextlib
import numpy as np
import concourse.bass as bass
import concourse.mybir as mybir
from concourse.bass_utils import run_bass_kernel_spmd

F32 = mybir.dt.float32
F32R = mybir.dt.float32r
BF16 = mybir.dt.bfloat16
I32 = mybir.dt.int32
AF = mybir.ActivationFunctionType
OP = mybir.AluOpType

B, SEQ, HID = 2, 4096, 4096
HEADS, HD, BLK = 32, 128, 256
NB = SEQ // BLK
INTER = 11008
NGK = HID // 128
NGI = INTER // 128
EPS = 1e-5

T_OWN = 1024
DEBUG = False
LAST_RESULTS = None
S_PRE = 3072
NB_PRE = S_PRE // BLK
NTH = T_OWN // 512
ISL = 256
NSL = INTER // ISL
NTC = T_OWN // 128


def _simm(v):
    v &= 0xFFFFFFFF
    return v - (1 << 32) if v >= (1 << 31) else v


def _decay_tables():
    f32 = np.float32
    base = f32(1.0) / f32(2.0) ** f32(8.0 / HEADS)
    exponent = np.arange(HEADS, dtype=np.float32) + f32(1.0)
    factor = f32(1.0 - 1 / (32 - 1 + 1e-5) + 1e-5)
    slope = (base ** exponent * factor).astype(np.float32)
    r = np.arange(BLK, dtype=np.float32) + f32(1.0)
    q_decay = np.exp(-slope[:, None] * r[None, :]).astype(np.float32)
    k_decay = np.exp(-slope[:, None] * (f32(BLK) - r)[None, :]).astype(np.float32)
    diff = r[:, None] - r[None, :]
    diag = slope[:, None, None] * diff[None, :, :]
    diag = np.where(diag >= 0, -diag, -np.inf).astype(np.float32)
    diag_decay = np.exp(diag).astype(np.float32)
    block_decay = np.exp(-slope * f32(BLK)).astype(np.float32)
    return slope, q_decay, k_decay, diag_decay, block_decay


def _tile_qw_gu(qw):
    q = qw.reshape(NGK, 16, INTER // 128, 128)
    return np.ascontiguousarray(q.transpose(1, 2, 0, 3).reshape(16, INTER // 128, NGK * 128))


def _tile_qw_d(qw):
    q = qw.reshape(NGI, 16, 8, 512)
    return np.ascontiguousarray(q.transpose(1, 2, 0, 3))


def _tile_w(wt):
    K, OF = wt.shape
    t = wt.reshape(K // 128, 128, OF // 128, 128)
    return np.ascontiguousarray(t.transpose(2, 1, 0, 3))


def _prep(inputs):
    for k, n in (("L0_ffn_gate_g_idx", HID), ("L0_ffn_up_g_idx", HID),
                 ("L0_ffn_down_g_idx", INTER)):
        gi = np.asarray(inputs[k])
        assert np.array_equal(gi, np.arange(n) // 128), f"unexpected {k}"
    x = np.ascontiguousarray(np.asarray(inputs["L0_attn_hidden_states"], dtype=np.float32))
    qkv_w = np.asarray(inputs["L0_attn_qkv_w"], dtype=np.float32)
    out_w = np.asarray(inputs["L0_attn_out_w"], dtype=np.float32)
    gate_w = np.asarray(inputs["L0_attn_gate_w"], dtype=np.float32)
    norm_w = np.asarray(inputs["L0_attn_norm_w"], dtype=np.float32)

    shared = {
        "wqkv": _tile_w(np.ascontiguousarray(qkv_w.T)),
        "wgate2": np.ascontiguousarray(gate_w.T),
        "wout": _tile_w(np.ascontiguousarray(out_w.T)),
        "normc": np.ascontiguousarray(norm_w.reshape(32, 128).T),
        "g_qw": _tile_qw_gu(np.asarray(inputs["L0_ffn_gate_qweight"], dtype=np.int32)),
        "g_s": np.ascontiguousarray(np.asarray(inputs["L0_ffn_gate_scales"], dtype=np.float32)),
        "g_qz": np.ascontiguousarray(np.asarray(inputs["L0_ffn_gate_qzeros"], dtype=np.int32)),
        "u_qw": _tile_qw_gu(np.asarray(inputs["L0_ffn_up_qweight"], dtype=np.int32)),
        "u_s": np.ascontiguousarray(np.asarray(inputs["L0_ffn_up_scales"], dtype=np.float32)),
        "u_qz": np.ascontiguousarray(np.asarray(inputs["L0_ffn_up_qzeros"], dtype=np.int32)),
        "d_qw": _tile_qw_d(np.asarray(inputs["L0_ffn_down_qweight"], dtype=np.int32)),
        "d_s": np.ascontiguousarray(np.asarray(inputs["L0_ffn_down_scales"], dtype=np.float32)),
        "d_qz": np.ascontiguousarray(np.asarray(inputs["L0_ffn_down_qzeros"], dtype=np.int32)),
    }

    _, q_decay, k_decay, diag_decay, _ = _decay_tables()
    shared["decayT"] = np.ascontiguousarray(diag_decay.transpose(0, 2, 1))
    qd = q_decay.reshape(HEADS, 2, 128).transpose(2, 0, 1).reshape(128, HEADS * 2)
    kd = k_decay.reshape(HEADS, 2, 128).transpose(2, 0, 1).reshape(128, HEADS * 2)
    shared["qdec"] = np.ascontiguousarray(qd)
    shared["kdec"] = np.ascontiguousarray(kd)
    j = np.arange(128) % 8
    shared["maskv"] = ((np.int64(15) << (4 * j)).astype(np.int64)
                       .astype(np.uint32).view(np.int32).reshape(128, 1).copy())
    shared["pow2v"] = (2.0 ** (-4.0 * j)).astype(np.float32).reshape(128, 1).copy()
    shared["pow2c"] = (2.0 ** (-4.0 * j)).astype(np.float32).reshape(1, 128).copy()
    shared["eye"] = np.eye(128, dtype=np.float32)
    ind = np.zeros((128, 2 * NGI - 1), dtype=np.float32)
    ind[:, NGI - 1] = 1.0
    shared["ind"] = ind

    per_core = []
    for c in range(8):
        s, p = c // 4, c % 4
        S = p * T_OWN
        x_seq = x[s]
        m = dict(shared)
        m["x_own"] = np.ascontiguousarray(x_seq[S:S + T_OWN])
        x_pre = np.zeros((S_PRE, HID), dtype=np.float32)
        if S > 0:
            x_pre[S_PRE - S:] = x_seq[:S]
        m["x_pre"] = x_pre
        per_core.append(m)
    return per_core


class Trk:
    def __init__(self, stack, nc, name):
        self.h = stack.enter_context(nc.semaphore(name))
        self.v = 0
        self.name = name

    def inc(self, instr, n=1):
        instr.then_inc(self.h, n)
        self.v += n
        return self.v


def build_nc():
    _, _, _, _, bdec = _decay_tables()
    nc = bass.Bass("TRN2", target_bir_lowering=False, debug=False)
    dt = lambda n, s, d, k="ExternalInput": nc.dram_tensor(n, s, d, kind=k).ap()

    x_own = dt("x_own", [T_OWN, HID], F32)
    x_pre = dt("x_pre", [S_PRE, HID], F32)
    wqkv = dt("wqkv", [96, 128, 32, 128], F32)
    wgate2 = dt("wgate2", [HID, HID], F32)
    wout = dt("wout", [32, 128, 32, 128], F32)
    normc = dt("normc", [128, 32], F32)
    g_qw = dt("g_qw", [16, INTER // 128, NGK * 128], I32)
    g_s = dt("g_s", [NGK, INTER], F32)
    g_qz = dt("g_qz", [NGK, INTER // 8], I32)
    u_qw = dt("u_qw", [16, INTER // 128, NGK * 128], I32)
    u_s = dt("u_s", [NGK, INTER], F32)
    u_qz = dt("u_qz", [NGK, INTER // 8], I32)
    d_qw = dt("d_qw", [16, 8, NGI, 512], I32)
    d_s = dt("d_s", [NGI, HID], F32)
    d_qz = dt("d_qz", [NGI, HID // 8], I32)
    decayT = dt("decayT", [HEADS, BLK, BLK], F32)
    qdec = dt("qdec", [128, HEADS * 2], F32)
    kdec = dt("kdec", [128, HEADS * 2], F32)
    maskv = dt("maskv", [128, 1], I32)
    pow2v = dt("pow2v", [128, 1], F32)
    pow2c = dt("pow2c", [1, 128], F32)
    eye = dt("eye", [128, 128], F32)
    ind = dt("ind", [128, 2 * NGI - 1], F32)
    y = dt("y", [T_OWN, HID], F32, "ExternalOutput")
    dbg = {}
    if DEBUG:
        dbg["d_xn"] = dt("d_xn", [T_OWN, HID], BF16, "ExternalOutput")
        dbg["d_qkv"] = dt("d_qkv", [96 * 128, T_OWN], BF16, "ExternalOutput")
        dbg["d_attn"] = dt("d_attn", [T_OWN, HID], BF16, "ExternalOutput")
        dbg["d_gate"] = dt("d_gate", [T_OWN, HID], BF16, "ExternalOutput")
        dbg["d_h"] = dt("d_h", [INTER, T_OWN], BF16, "ExternalOutput")
        dbg["d_sg"] = dt("d_sg", [HID, T_OWN], BF16, "ExternalOutput")
        dbg["d_rstd"] = dt("d_rstd", [T_OWN, 1], F32, "ExternalOutput")

    si = lambda n, s, d: nc.dram_tensor(n, s, d, kind="Internal").ap()
    xn_own_sc = si("xn_own_sc", [T_OWN, HID], BF16)
    xn_pre_sc = si("xn_pre_sc", [S_PRE, HID], BF16)
    qkv_sc = si("qkv_sc", [96 * 128, SEQ], BF16)
    attn_sc = si("attn_sc", [T_OWN, HID], BF16)
    gate_sc = si("gate_sc", [T_OWN, HID], BF16)
    h_sc = si("h_sc", [INTER, T_OWN], BF16)
    sg_sc = si("sg_sc", [HID, T_OWN], BF16)
    rstd_sc = si("rstd_sc", [T_OWN, 1], F32)
    gsb_sc = si("gsb_sc", [NGK, INTER], BF16)
    usb_sc = si("usb_sc", [NGK, INTER], BF16)
    dsb_sc = si("dsb_sc", [NGI, HID], BF16)
    szg_sc = si("szg_sc", [NGK, INTER], BF16)
    szu_sc = si("szu_sc", [NGK, INTER], BF16)

    stack = contextlib.ExitStack()
    with stack:
        sb = lambda n, s, d, **kw: stack.enter_context(nc.sbuf_tensor(n, s, d, **kw))
        psb = [stack.enter_context(nc.psum_tensor(f"ps{i}", [128, 512], F32))
               for i in range(8)]

        t_qdec = sb("t_qdec", [128, HEADS * 2], F32)
        t_kdec = sb("t_kdec", [128, HEADS * 2], F32)
        t_mask = sb("t_mask", [128, 1], I32)
        t_pow2 = sb("t_pow2", [128, 1], F32)
        t_p2cr = sb("t_p2cr", [1, 128], F32R)
        t_eye = sb("t_eye", [128, 128], BF16)
        t_one = sb("t_one", [1, 128], BF16)
        t_norm = sb("t_norm", [128, 32], F32)
        t_ind = sb("t_ind", [128, 2 * NGI - 1], BF16)
        t_const = sb("t_const", [128, 2], F32)

        s_wl = Trk(stack, nc, "s_wl")
        s_al = Trk(stack, nc, "s_al")
        s_tl = Trk(stack, nc, "s_tl")
        s_st = Trk(stack, nc, "s_st")
        s_pe = Trk(stack, nc, "s_pe")
        s_act = Trk(stack, nc, "s_act")
        s_dve = Trk(stack, nc, "s_dve")
        s_pl = Trk(stack, nc, "s_pl")
        s_out = Trk(stack, nc, "s_out")
        ALL_TRK = [s_wl, s_al, s_tl, s_st, s_pe, s_act, s_dve, s_pl, s_out]

        wait_cache = {}

        def W(eng, trk, v=None):
            v = trk.v if v is None else v
            if v <= 0:
                return
            key = (id(eng), trk.name)
            if wait_cache.get(key, 0) >= v:
                return
            eng.wait_ge(trk.h, v)
            wait_cache[key] = v

        def dma(eng, out, in_, trk, **kw):
            eng.dma_start(out=out, in_=in_, **kw).then_inc(trk.h, 16)
            trk.v += 16
            return trk.v

        SY, SC, VE, PE, GP = nc.sync, nc.scalar, nc.vector, nc.tensor, nc.gpsimd
        ENGINES = [SY, SC, VE, PE, GP]

        def barrier():
            for e in ENGINES:
                for t in ALL_TRK:
                    W(e, t)

        dma(SY, t_qdec[:, :], qdec, s_al)
        dma(SY, t_kdec[:, :], kdec, s_al)
        dma(SY, t_mask[:, :], maskv, s_al)
        dma(SY, t_pow2[:, :], pow2v, s_al)
        dma(SY, t_p2cr[:, :], pow2c.bitcast(F32R), s_al)
        dma(GP, t_eye[:, :], eye, s_wl)
        dma(SY, t_norm[:, :], normc, s_al)
        dma(GP, t_ind[:, :], ind, s_wl)
        scale_mark = s_wl.v
        s_dve.inc(VE.memset(t_one[:, :], 1.0))
        s_dve.inc(VE.memset(t_const[:, 0:1], EPS))
        s_dve.inc(VE.memset(t_const[:, 1:2], 1.0 / HID))

        st_sz = contextlib.ExitStack()
        t_szd = st_sz.enter_context(nc.sbuf_tensor("t_szd", [NGI, HID], BF16))
        t_hg = st_sz.enter_context(nc.sbuf_tensor("t_hg", [NGI, T_OWN], BF16))
        with contextlib.ExitStack() as pre:
            sbl = lambda n, s, d: pre.enter_context(nc.sbuf_tensor(n, s, d))
            t_zi = sbl("t_zi", [NGI, INTER], I32)
            t_zf = sbl("t_zf", [NGI, INTER], F32)
            t_qzb = sbl("t_qzb", [NGI, INTER // 8], I32)
            t_sf = sbl("t_sf", [NGI, INTER], F32)
            t_szb = sbl("t_szb", [NGI, INTER], BF16)
            for (qz_d, s_d, spill_to, ng, width) in (
                (g_qz, g_s, szg_sc, NGK, INTER),
                (u_qz, u_s, szu_sc, NGK, INTER),
                (d_qz, d_s, None, NGI, HID),
            ):
                w8 = width // 8
                W(SY, s_dve)
                W(SY, s_st)
                dma(SY, t_qzb[0:ng, 0:w8], qz_d, s_al)
                dma(SY, t_sf[0:ng, 0:width], s_d, s_al)
                W(VE, s_al)
                for jn in range(8):
                    VE.tensor_scalar(t_zi[0:ng, jn:width:8], t_qzb[0:ng, 0:w8],
                                     _simm(15 << (4 * jn)), None, OP.bitwise_and)
                for jn in range(8):
                    VE.tensor_scalar(t_zf[0:ng, jn:width:8], t_zi[0:ng, jn:width:8],
                                     float(2.0 ** (-4 * jn)), None, OP.mult)
                VE.tensor_scalar(t_zf[0:ng, 0:width], t_zf[0:ng, 0:width],
                                 1.0, None, OP.add)
                tgt = t_szd if spill_to is None else t_szb
                iv = VE.scalar_tensor_tensor(tgt[0:ng, 0:width], t_zf[0:ng, 0:width],
                                             -1.0, t_sf[0:ng, 0:width],
                                             OP.mult, OP.mult)
                s_dve.inc(iv)
                if spill_to is not None:
                    W(SY, s_dve)
                    dma(SY, spill_to, t_szb[0:ng, 0:width], s_st)
            barrier()
        sz_spill_mark = s_st.v

        NTILE = NTC + S_PRE // 128
        with contextlib.ExitStack() as p0:
            sbl = lambda n, s, d, **k: p0.enter_context(nc.sbuf_tensor(n, s, d, **k))
            xb = [sbl(f"p0x{i}", [128, HID], F32) for i in range(2)]
            dummy = sbl("p0dummy", [128, HID], F32)
            xnb = [sbl(f"p0n{i}", [128, HID], BF16) for i in range(2)]
            st_sum = sbl("p0sum", [128, NTILE], F32)
            st_ss = sbl("p0ss", [128, NTILE], F32)
            st_t = sbl("p0t", [128, NTILE], F32)
            tiles = [(x_own[t * 128:(t + 1) * 128, :],
                      xn_own_sc[t * 128:(t + 1) * 128, :]) for t in range(NTC)]
            tiles += [(x_pre[t * 128:(t + 1) * 128, :],
                       xn_pre_sc[t * 128:(t + 1) * 128, :]) for t in range(S_PRE // 128)]
            act_war = [0, 0]
            for i, (src, dst) in enumerate(tiles):
                b = i % 2
                W(SY, s_act, act_war[b])
                dma(SY, xb[b][:, :], src, s_al)
                W(SC, s_al)
                SC.activation(dummy[:, :], xb[b][:, :], AF.Copy,
                              accum_out=st_sum[:, i:i + 1])
                i1 = SC.activation(dummy[:, :], xb[b][:, :], AF.Square,
                                   accum_out=st_ss[:, i:i + 1])
                act_war[b] = s_act.inc(i1)
            W(VE, s_act)
            iv = VE.tensor_scalar(st_sum[:, :], st_sum[:, :], -1.0 / HID, None,
                                  OP.mult)
            s_dve.inc(iv)
            W(VE, s_dve)
            iv = VE.tensor_tensor(st_t[:, :], st_sum[:, :], st_sum[:, :], OP.mult)
            s_dve.inc(iv)
            W(VE, s_dve)
            iv = VE.scalar_tensor_tensor(st_ss[:, :], st_ss[:, :], 1.0 / HID,
                                         st_t[:, :], OP.mult, OP.subtract)
            s_dve.inc(iv)
            W(SC, s_dve)
            i3 = SC.activation(st_ss[:, :], st_ss[:, :], AF.Sqrt,
                               bias=t_const[:, 0:1])
            s_act.inc(i3)
            W(VE, s_act)
            iv = VE.reciprocal(st_ss[:, :], st_ss[:, :])
            s_dve.inc(iv)
            W(VE, s_dve)
            nrm_war = [0, 0]
            sp_war = [0, 0]
            for i, (src, dst) in enumerate(tiles):
                b = i % 2
                W(SY, s_dve, nrm_war[b])
                dma(SY, xb[b][:, :], src, s_al)
                W(VE, s_al)
                W(VE, s_st, sp_war[b])
                iv = VE.tensor_scalar(xnb[b][:, :], xb[b][:, :],
                                      st_sum[:, i:i + 1], st_ss[:, i:i + 1],
                                      OP.add, OP.mult)
                nrm_war[b] = s_dve.inc(iv)
                W(SY, s_dve)
                sp_war[b] = dma(SY, dst, xnb[b][:, :], s_st)
            barrier()
        p0_spill_done = s_st.v

        kv_chunks = [c for c in range(96) if c % 3 != 0]
        FH_T = S_PRE // 2
        FH_TILES = FH_T // 512

        pstate = {"war_pe": [0, 0], "war_act": [0, 0], "war_sp": [0, 0]}

        def proj_pass(wb, evb, xn_tile_ap, ntile, chunk_list, wdram, act_fn,
                      spill_fn, bank_groups):
            for idx, ofc in enumerate(chunk_list):
                bwt = idx % 2
                W(GP, s_pe, pstate["war_pe"][bwt])
                dma(GP, wb[bwt][:, :], wdram[ofc], s_wl)
                g = idx % 2
                banks = bank_groups[g]
                W(PE, s_wl)
                W(PE, s_act, pstate["war_act"][g])
                mm = None
                for c in range(32):
                    for t in range(ntile):
                        mm = PE.matmul(psb[banks[t]][:, :],
                                       wb[bwt][:, c * 128:(c + 1) * 128],
                                       xn_tile_ap(c, t),
                                       start=(c == 0), stop=(c == 31))
                pstate["war_pe"][bwt] = s_pe.inc(mm)
                W(SC, s_pe)
                W(SC, s_st, pstate["war_sp"][g])
                ia = None
                for t in range(ntile):
                    ia = SC.activation(evb[g][:, t * 512:(t + 1) * 512],
                                       psb[banks[t]][:, :], act_fn)
                pstate["war_act"][g] = s_act.inc(ia)
                W(SY, s_act)
                pstate["war_sp"][g] = dma(SY, spill_fn(ofc, ntile),
                                          evb[g][:, 0:ntile * 512], s_st)

        st14 = contextlib.ExitStack()
        with contextlib.ExitStack() as p1:
            sbl = lambda n, s, d, **k: p1.enter_context(nc.sbuf_tensor(n, s, d, **k))
            wb = [sbl(f"p1w{i}", [128, 4096], BF16) for i in range(2)]
            evb = [sbl(f"p1e{i}", [128, 3 * 512], BF16) for i in range(2)]

            ttile = [p1.enter_context(nc.sbuf_tensor(f"p1t{i}", [128, HID], BF16))
                     for i in range(2)]
            psbf67 = [psb[6].bitcast(BF16), psb[7].bitcast(BF16)]
            tr_state = {"tl_war": [0, 0], "bank_war": [0, 0]}

            def build_fm(dst_ap_fn, src_sc, row0, ntc):
                W(PE, s_wl)
                for tc in range(ntc):
                    tb = tc % 2
                    W(SY, s_pe, tr_state["tl_war"][tb])
                    dma(SY, ttile[tb][:, :],
                        src_sc[row0 + tc * 128:row0 + (tc + 1) * 128, :], s_al)
                    W(PE, s_al)
                    for cg in range(8):
                        bk = cg % 2
                        W(PE, s_dve, tr_state["bank_war"][bk])
                        it = None
                        for ci in range(4):
                            c = cg * 4 + ci
                            it = PE.transpose(psbf67[bk][:, ci * 128:(ci + 1) * 128],
                                              ttile[tb][:, c * 128:(c + 1) * 128],
                                              t_eye[:, :])
                        mtr = s_pe.inc(it)
                        W(VE, s_pe, mtr)
                        iv = None
                        for ci in range(4):
                            c = cg * 4 + ci
                            iv = VE.tensor_copy(dst_ap_fn(c, tc),
                                                psbf67[bk][:, ci * 128:(ci + 1) * 128])
                        tr_state["bank_war"][bk] = s_dve.inc(iv)
                    tr_state["tl_war"][tb] = s_pe.v
                W(PE, s_dve)

            with contextlib.ExitStack() as p1f:
                xnf = p1f.enter_context(
                    nc.sbuf_tensor("p1xnf", [128, 32 * FH_T], BF16,
                                   is_transpose_dest=True))
                for fh in range(2):
                    W(SY, s_st, p0_spill_done)
                    if fh > 0:
                        W(SY, s_pe)
                    build_fm(lambda c, tc: xnf[:, c * FH_T + tc * 128:c * FH_T + (tc + 1) * 128],
                             xn_pre_sc, fh * FH_T, FH_T // 128)
                    col0 = fh * FH_T

                    def spill_kv(ofc, ntile, col0=col0):
                        return qkv_sc[ofc * 128:(ofc + 1) * 128,
                                      col0:col0 + ntile * 512]

                    proj_pass(wb, evb,
                              lambda c, t: xnf[:, c * FH_T + t * 512:c * FH_T + (t + 1) * 512],
                              FH_TILES, kv_chunks, wqkv, AF.Silu, spill_kv,
                              [(0, 1, 2), (3, 4, 5)])

            xno = st14.enter_context(
                nc.sbuf_tensor("xno", [128, 32 * T_OWN], BF16,
                               is_transpose_dest=True, side="right"))
            W(SY, s_st, p0_spill_done)
            W(SY, s_pe)
            build_fm(lambda c, tc: xno[:, c * T_OWN + tc * 128:c * T_OWN + (tc + 1) * 128],
                     xn_own_sc, 0, NTC)

            def spill_own(ofc, ntile):
                return qkv_sc[ofc * 128:(ofc + 1) * 128, S_PRE:S_PRE + ntile * 512]

            proj_pass(wb, evb,
                      lambda c, t: xno[:, c * T_OWN + t * 512:c * T_OWN + (t + 1) * 512],
                      NTH, list(range(96)), wqkv, AF.Silu, spill_own,
                      [(0, 1), (2, 3)])
        p1_done_st = s_st.v
        barrier()

        with contextlib.ExitStack() as p2:
            sbl = lambda n, s, d, **k: p2.enter_context(nc.sbuf_tensor(n, s, d, **k))
            sdt = sbl("sdt", [128, HEADS * 2 * 256], F32)
            lkv = sbl("lkv", [128, HEADS * 128], F32)
            lkv16 = sbl("lkv16", [128, HEADS * 128], BF16)
            mgrp = [sbl(f"mgrp{i}", [128, 12 * 256], BF16) for i in range(2)]
            mvt = [sbl(f"mvt{i}", [128, 256], BF16) for i in range(2)]
            khat = [sbl(f"khat{i}", [128, 256], BF16) for i in range(2)]
            scd = [sbl(f"scd{i}", [128, 512], BF16) for i in range(2)]
            t1b = [sbl(f"t1b{i}", [128, 256], F32) for i in range(2)]
            attn_blk = sbl("attn_blk", [128, 2 * HID], BF16)

            for h in range(HEADS):
                for half in range(2):
                    dma(SY, sdt[:, (h * 2 + half) * 256:(h * 2 + half + 1) * 256],
                        decayT[h, half * 128:(half + 1) * 128, :], s_al)
            VE.memset(lkv[:, :], 0.0)
            s_dve.inc(VE.memset(lkv16[:, :], 0.0))

            unit_dve = {}
            pe_done = {}
            dve_done = {}
            grp_pe = [0, 0]
            attn_war = 0
            for n in range(NB):
                owned = n >= NB_PRE
                for hg in range(8):
                    gidx = n * 8 + hg
                    gu = gidx % 2
                    W(SY, s_pe, grp_pe[gu])
                    W(SY, s_st, p1_done_st)
                    dma(SY, mgrp[gu][:, :],
                        qkv_sc[hg * 12 * 128:(hg + 1) * 12 * 128,
                               n * 256:(n + 1) * 256]
                        .rearrange("(c p) w -> p c w", p=128), s_al)
                    grp_mark = s_al.v
                    for hh in range(4):
                        h = hg * 4 + hh
                        uidx = n * HEADS + h
                        u = uidx % 2
                        prev_pe = pe_done.get(uidx - 2, 0)
                        prev_dve = dve_done.get(uidx - 2, 0)
                        q_fm = mgrp[gu][:, (hh * 3) * 256:(hh * 3 + 1) * 256]
                        k_fm = mgrp[gu][:, (hh * 3 + 1) * 256:(hh * 3 + 2) * 256]
                        v_fm = mgrp[gu][:, (hh * 3 + 2) * 256:(hh * 3 + 3) * 256]
                        bs, bo, bi, bu = (0, 1, 2, 3) if u == 0 else (4, 5, 6, 7)
                        psT = psb[bs].bitcast(BF16)
                        W(PE, s_al, grp_mark)
                        W(PE, s_dve, prev_dve)
                        it = None
                        for half in range(2):
                            it = PE.transpose(psT[:, half * 128:(half + 1) * 128],
                                              k_fm[:, half * 128:(half + 1) * 128],
                                              t_eye[:, :])
                            it = PE.transpose(psT[:, (2 + half) * 128:(3 + half) * 128],
                                              v_fm[:, half * 128:(half + 1) * 128],
                                              t_eye[:, :])
                        tmark = s_pe.inc(it)
                        W(VE, s_pe, tmark)
                        W(VE, s_pe, prev_pe)
                        iv = None
                        for half in range(2):
                            VE.tensor_scalar(
                                khat[u][:, half * 128:(half + 1) * 128],
                                psT[:, half * 128:(half + 1) * 128],
                                t_kdec[:, h * 2 + half:h * 2 + half + 1], None,
                                OP.mult)
                            iv = VE.tensor_copy(
                                mvt[u][:, half * 128:(half + 1) * 128],
                                psT[:, (2 + half) * 128:(3 + half) * 128])
                        evac_mark = s_dve.inc(iv)
                        if owned:
                            W(PE, s_dve, evac_mark)
                            ip = None
                            for half in range(2):
                                ip = PE.matmul(psb[bs][:, half * 256:(half + 1) * 256],
                                               k_fm[:, half * 128:(half + 1) * 128],
                                               q_fm, start=True, stop=True)
                            sc_mark = s_pe.inc(ip)
                            W(VE, s_pe, sc_mark)
                            ivv = None
                            for half in range(2):
                                ivv = VE.tensor_tensor(
                                    scd[u][:, half * 256:(half + 1) * 256],
                                    psb[bs][:, half * 256:(half + 1) * 256],
                                    sdt[:, (h * 2 + half) * 256:(h * 2 + half + 1) * 256],
                                    OP.mult)
                            scd_mark = s_dve.inc(ivv)
                            W(PE, s_dve, scd_mark)
                            W(PE, s_dve, unit_dve.get((n - 1, h), 0))
                            for tqh in range(2):
                                PE.matmul(psb[bo][:, tqh * 128:(tqh + 1) * 128],
                                          scd[u][:, tqh * 128:tqh * 128 + 128],
                                          mvt[u][:, 0:128], start=True, stop=False)
                                PE.matmul(psb[bo][:, tqh * 128:(tqh + 1) * 128],
                                          scd[u][:, 256 + tqh * 128:256 + tqh * 128 + 128],
                                          mvt[u][:, 128:256], start=False, stop=True)
                                PE.matmul(psb[bi][:, tqh * 128:(tqh + 1) * 128],
                                          q_fm[:, tqh * 128:(tqh + 1) * 128],
                                          lkv16[:, h * 128:(h + 1) * 128],
                                          start=True, stop=True)
                        else:
                            W(PE, s_dve, evac_mark)
                        W(PE, s_dve, prev_dve)
                        PE.matmul(psb[bu][:, 0:128], khat[u][:, 0:128],
                                  mvt[u][:, 0:128], start=True, stop=False)
                        ip2 = PE.matmul(psb[bu][:, 0:128], khat[u][:, 128:256],
                                        mvt[u][:, 128:256], start=False, stop=True)
                        pe_done[uidx] = s_pe.inc(ip2)
                        W(VE, s_pe, pe_done[uidx])
                        if owned:
                            W(VE, s_st, attn_war)
                            it1 = None
                            for tqh in range(2):
                                it1 = VE.tensor_scalar(
                                    t1b[u][:, tqh * 128:(tqh + 1) * 128],
                                    psb[bi][:, tqh * 128:(tqh + 1) * 128],
                                    t_qdec[:, h * 2 + tqh:h * 2 + tqh + 1], None,
                                    OP.mult)
                            s_dve.inc(it1)
                            W(VE, s_dve)
                            for tqh in range(2):
                                VE.tensor_tensor(
                                    attn_blk[:, tqh * HID + h * 128:tqh * HID + (h + 1) * 128],
                                    psb[bo][:, tqh * 128:(tqh + 1) * 128],
                                    t1b[u][:, tqh * 128:(tqh + 1) * 128], OP.add)
                        ist = VE.scalar_tensor_tensor(lkv[:, h * 128:(h + 1) * 128],
                                                      lkv[:, h * 128:(h + 1) * 128],
                                                      float(bdec[h]),
                                                      psb[bu][:, 0:128],
                                                      OP.mult, OP.add)
                        s_dve.inc(ist)
                        W(VE, s_dve)
                        iu = VE.tensor_copy(lkv16[:, h * 128:(h + 1) * 128],
                                            lkv[:, h * 128:(h + 1) * 128])
                        dve_done[uidx] = s_dve.inc(iu)
                        unit_dve[(n, h)] = dve_done[uidx]
                    grp_pe[gu] = s_pe.v
                if owned:
                    W(SY, s_dve)
                    r0 = (n - NB_PRE) * 256
                    for tqh in range(2):
                        attn_war = dma(
                            SY, attn_sc[r0 + tqh * 128:r0 + (tqh + 1) * 128, :],
                            attn_blk[:, tqh * HID:(tqh + 1) * HID], s_st)
        barrier()

        with contextlib.ExitStack() as p3:
            sbl = lambda n, s, d, **k: p3.enter_context(nc.sbuf_tensor(n, s, d, **k))
            wg2b = [sbl(f"p3w{i}", [128, 32 * 512], BF16) for i in range(2)]
            gt_t = [sbl(f"p3e{i}", [128, 512], BF16) for i in range(2)]
            w_war = [0, 0]
            e_war = [0, 0]
            sp_war = [0, 0]
            for of in range(8):
                b = of % 2
                W(GP, s_pe, w_war[b])
                dma(GP, wg2b[b][:, :],
                    wgate2[:, of * 512:(of + 1) * 512]
                    .rearrange("(c k) o -> k c o", k=128), s_wl)
                W(PE, s_wl)
                W(PE, s_act)
                mm = None
                for c in range(32):
                    for tc in range(NTC):
                        mm = PE.matmul(psb[tc][:, :],
                                       xno[:, c * T_OWN + tc * 128:c * T_OWN + (tc + 1) * 128],
                                       wg2b[b][:, c * 512:(c + 1) * 512],
                                       start=(c == 0), stop=(c == 31))
                w_war[b] = s_pe.inc(mm)
                W(SC, s_pe)
                for tc in range(NTC):
                    eb = tc % 2
                    W(SC, s_st, sp_war[eb])
                    ia = SC.activation(gt_t[eb][:, :], psb[tc][:, :], AF.Sigmoid)
                    s_act.inc(ia)
                    W(SY, s_act)
                    sp_war[eb] = dma(
                        SY, gate_sc[tc * 128:(tc + 1) * 128, of * 512:(of + 1) * 512],
                        gt_t[eb][:, :], s_st)
        barrier()

        P4SL = 128
        P4N = INTER // P4SL
        with contextlib.ExitStack() as p4:
            sbl = lambda n, s, d, **k: p4.enter_context(nc.sbuf_tensor(n, s, d, **k))
            t_xg = sbl("t_xg", [NGK, T_OWN], BF16)
            rep_all = {m: sbl(f"p4r{m}", [128, NGK * P4SL], I32) for m in "gu"}
            sbc_all = {m: sbl(f"p4b{m}", [128, NGK * P4SL], F32) for m in "gu"}
            dq = {m: [sbl(f"p4d{m}{i}", [128, NGK * P4SL], BF16) for i in range(2)]
                  for m in "gu"}
            szb = {m: [sbl(f"p4z{m}{i}", [NGK, P4SL], BF16) for i in range(2)]
                   for m in "gu"}
            h_t = [sbl(f"p4h{i}", [128, T_OWN], BF16) for i in range(2)]
            sil = [sbl(f"p4sil{i}", [128, 512], BF16) for i in range(2)]

            for th in range(NTH):
                mm = None
                for c in range(32):
                    mm = PE.matmul(psb[6 + th][0:NGK, :],
                                   t_ind[:, (NGI - 1 - c):(NGI - 1 - c) + NGK],
                                   xno[:, c * T_OWN + th * 512:c * T_OWN + (th + 1) * 512],
                                   start=(c == 0), stop=(c == 31))
                s_pe.inc(mm)
            W(SC, s_pe)
            ia = None
            for th in range(NTH):
                ia = SC.activation(t_xg[0:NGK, th * 512:(th + 1) * 512],
                                   psb[6 + th][0:NGK, :], AF.Copy)
            xg_mark = s_act.inc(ia)
            W(PE, s_act, xg_mark)

            qw_d = {"g": g_qw, "u": u_qw}
            sc_d = {"g": g_s, "u": u_s}
            sz_sc = {"g": szg_sc, "u": szu_sc}

            stage_done = [0] * P4N
            sz_mark = [0] * P4N
            tt_done = {m: 0 for m in "gu"}
            bc_war = {4: 0, 5: 0}
            srow_pe = {m: 0 for m in "gu"}
            dq_war = [0, 0]
            szb_war = [0, 0]
            h_spill_war = [0, 0]
            hg_war = [0, 0]
            sil_war = [0, 0]
            ld_mark4 = [0] * P4N

            def stage_dma(j):
                sj = j % 2
                c0, c1 = j * P4SL, (j + 1) * P4SL
                for m in "gu":
                    W(SY, s_dve, tt_done[m])
                    dma(SY, rep_all[m][:, :],
                        qw_d[m][:, j, :].unsqueeze(1)
                        .broadcast_to([16, 8, NGK * P4SL]), s_al)
                    dma(SY, sbc_all[m][:, :],
                        sc_d[m][:, c0:c1].unsqueeze(0)
                        .broadcast_to([128, NGK, P4SL]), s_al)
                    W(SY, s_pe, szb_war[sj])
                    W(SY, s_st, sz_spill_mark)
                    dma(SY, szb[m][sj][0:NGK, :], sz_sc[m][:, c0:c1], s_al)
                ld_mark4[j] = s_al.v

            def stage_compute(j):
                sj = j % 2
                for m in "gu":
                    W(VE, s_al, ld_mark4[j])
                    W(VE, s_pe, dq_war[sj])
                    ig = VE.tensor_scalar(rep_all[m][:, :], rep_all[m][:, :],
                                          t_mask[:, 0:1], None, OP.bitwise_and)
                    s_dve.inc(ig)
                    im = VE.scalar_tensor_tensor(dq[m][sj][:, :], rep_all[m][:, :],
                                                 t_pow2[:, 0:1], sbc_all[m][:, :],
                                                 OP.mult, OP.mult)
                    tt_done[m] = s_dve.inc(im)
                stage_done[j] = s_dve.v
                sz_mark[j] = ld_mark4[j]

            def main_th(j, th):
                sj = j % 2
                gi = j
                W(PE, s_dve, stage_done[j])
                W(PE, s_al, sz_mark[j])
                for g in range(NGK):
                    PE.matmul(psb[th][:, :],
                              dq["g"][sj][:, g * P4SL:(g + 1) * P4SL],
                              xno[:, g * T_OWN + th * 512:g * T_OWN + (th + 1) * 512],
                              start=(g == 0), stop=False)
                    PE.matmul(psb[2 + th][:, :],
                              dq["u"][sj][:, g * P4SL:(g + 1) * P4SL],
                              xno[:, g * T_OWN + th * 512:g * T_OWN + (th + 1) * 512],
                              start=(g == 0), stop=False)
                W(PE, s_act, xg_mark)
                PE.matmul(psb[th][:, :], szb["g"][sj][0:NGK, :],
                          t_xg[0:NGK, th * 512:(th + 1) * 512],
                          start=False, stop=True)
                mmu = PE.matmul(psb[2 + th][:, :], szb["u"][sj][0:NGK, :],
                                t_xg[0:NGK, th * 512:(th + 1) * 512],
                                start=False, stop=True)
                gu_mark = s_pe.inc(mmu)
                W(SC, s_pe, gu_mark)
                W(SC, s_dve, sil_war[th])
                isl_ = SC.activation(sil[th][:, :], psb[th][:, :], AF.Silu)
                smark = s_act.inc(isl_)
                hb = gi % 2
                W(VE, s_act, smark)
                W(VE, s_st, h_spill_war[hb])
                W(VE, s_pe, hg_war[hb])
                ih = VE.tensor_tensor(h_t[hb][:, th * 512:(th + 1) * 512],
                                      sil[th][:, :], psb[2 + th][:, :], OP.mult)
                h_mark = s_dve.inc(ih)
                sil_war[th] = h_mark
                W(PE, s_dve, h_mark)
                ihg = PE.matmul(psb[6 + th][0:NGI, :],
                                t_ind[:, (NGI - 1 - gi):(NGI - 1 - gi) + NGI],
                                h_t[hb][:, th * 512:(th + 1) * 512],
                                start=(gi == 0), stop=(gi == P4N - 1))
                hg_war[hb] = s_pe.inc(ihg)

            stage_dma(0)
            stage_compute(0)
            for j in range(P4N):
                gi = j
                hb = gi % 2
                if j + 1 < P4N:
                    stage_dma(j + 1)
                main_th(j, 0)
                if j + 1 < P4N:
                    stage_compute(j + 1)
                main_th(j, 1)
                W(SY, s_dve)
                h_spill_war[hb] = dma(SY, h_sc[gi * 128:(gi + 1) * 128, :],
                                      h_t[hb][:, :], s_st)
                sj = j % 2
                dq_war[sj] = s_pe.v
                szb_war[sj] = s_pe.v
            W(SC, s_pe)
            ia = None
            for th in range(NTH):
                ia = SC.activation(t_hg[0:NGI, th * 512:(th + 1) * 512],
                                   psb[6 + th][0:NGI, :], AF.Copy)
            s_act.inc(ia)
        st14.close()
        barrier()

        st56 = contextlib.ExitStack()
        t_gfm = st56.enter_context(
            nc.sbuf_tensor("t_gfm", [128, 32 * T_OWN], BF16, side="right"))
        GB = 8
        gblocks = [(g0, min(GB, NGI - g0)) for g0 in range(0, NGI, GB)]
        with contextlib.ExitStack() as p5:
            sbl = lambda n, s, d, **k: p5.enter_context(nc.sbuf_tensor(n, s, d, **k))
            repd = [sbl(f"p5r{i}", [128, GB * 512], I32) for i in range(2)]
            sbcd = [sbl(f"p5s{i}", [128, GB * 512], F32) for i in range(2)]
            dqd = [sbl(f"p5w{i}", [128, GB * 512], BF16) for i in range(2)]
            hb = [sbl(f"p5h{i}", [128, GB * T_OWN], BF16) for i in range(2)]
            iters = [(og, g0, gn) for og in range(8) for (g0, gn) in gblocks]
            tt5_war = [0, 0]
            dq_war5 = [0, 0]
            h_war5 = [0, 0]
            ld5 = [0, 0]

            def p5_dma(idx):
                og, g0, gn = iters[idx]
                bb = idx % 2
                c0 = og * 512
                W(SY, s_dve, tt5_war[bb])
                dma(SY, repd[bb][:, 0:gn * 512],
                    d_qw[:, og, g0:g0 + gn, :].unsqueeze(1)
                    .broadcast_to([16, 8, gn, 512]), s_al)
                dma(SY, sbcd[bb][:, 0:gn * 512],
                    d_s[g0:g0 + gn, c0:c0 + 512].unsqueeze(0)
                    .broadcast_to([128, gn, 512]), s_al)
                W(SY, s_pe, h_war5[bb])
                dma(SY, hb[bb][:, 0:gn * T_OWN],
                    h_sc[g0 * 128:(g0 + gn) * 128, :]
                    .rearrange("(g p) t -> p g t", p=128), s_al)
                ld5[bb] = s_al.v

            p5_dma(0)
            for idx, (og, g0, gn) in enumerate(iters):
                bb = idx % 2
                c0 = og * 512
                if idx + 1 < len(iters):
                    p5_dma(idx + 1)
                W(VE, s_al, ld5[bb])
                W(VE, s_pe, dq_war5[bb])
                VE.tensor_scalar(repd[bb][:, 0:gn * 512], repd[bb][:, 0:gn * 512],
                                 t_mask[:, 0:1], None, OP.bitwise_and)
                im = VE.scalar_tensor_tensor(dqd[bb][:, 0:gn * 512],
                                             repd[bb][:, 0:gn * 512],
                                             t_pow2[:, 0:1],
                                             sbcd[bb][:, 0:gn * 512],
                                             OP.mult, OP.mult)
                tt5_war[bb] = s_dve.inc(im)
                W(PE, s_dve, tt5_war[bb])
                W(PE, s_al, ld5[bb])
                if og > 0 and g0 == 0:
                    W(PE, s_act)
                mm = None
                for g in range(gn):
                    for ofc in range(4):
                        for th in range(NTH):
                            mm = PE.matmul(
                                psb[ofc * 2 + th][:, :],
                                dqd[bb][:, g * 512 + ofc * 128:g * 512 + (ofc + 1) * 128],
                                hb[bb][:, g * T_OWN + th * 512:g * T_OWN + (th + 1) * 512],
                                start=(g0 + g == 0), stop=False)
                pe5 = s_pe.inc(mm)
                dq_war5[bb] = pe5
                h_war5[bb] = pe5
                if g0 + gn == NGI:
                    mm = None
                    for ofc in range(4):
                        for th in range(NTH):
                            mm = PE.matmul(psb[ofc * 2 + th][:, :],
                                           t_szd[0:NGI, c0 + ofc * 128:c0 + (ofc + 1) * 128],
                                           t_hg[0:NGI, th * 512:(th + 1) * 512],
                                           start=False, stop=True)
                    dn_mark = s_pe.inc(mm)
                    W(SC, s_pe, dn_mark)
                    ia = None
                    for ofc in range(4):
                        ofc_g = og * 4 + ofc
                        for th in range(NTH):
                            ia = SC.activation(
                                t_gfm[:, ofc_g * T_OWN + th * 512:ofc_g * T_OWN + (th + 1) * 512],
                                psb[ofc * 2 + th][:, :], AF.Silu)
                    s_act.inc(ia)
        st_sz.close()
        barrier()

        with contextlib.ExitStack() as p6:
            sbl = lambda n, s, d, **k: p6.enter_context(nc.sbuf_tensor(n, s, d, **k))
            ab2 = [sbl(f"p6a{i}", [128, HID], BF16) for i in range(2)]
            gt2 = [sbl("p6g0", [128, HID], BF16)] * 2
            prod_tm = [sbl("p6pt0", [128, HID], BF16)] * 2
            ss2 = sbl("p6ss", [128, NTC], F32)
            t_prod = sbl("p6p", [128, 32 * T_OWN], BF16, side="right")
            wob = [sbl(f"p6w{i}", [128, 4096], BF16) for i in range(2)]
            sgt = [sbl(f"p6sg{i}", [128, T_OWN], BF16) for i in range(2)]
            sgc = sbl("p6sc", [128, 32 * 128], BF16)
            xb2 = sbl("p6x", [128, HID], F32)
            dummy2 = xb2
            psbf01 = [psb[0].bitcast(BF16), psb[1].bitcast(BF16)]

            ld_war = [0, 0]
            bank_war6 = [0, 0]
            p6_prod_pe = [0]
            prev_mark = [0]
            for tc in range(NTC):
                b = tc % 2
                W(SY, s_dve, ld_war[b])
                W(SY, s_dve, prev_mark[0])
                dma(SY, ab2[b][:, :], attn_sc[tc * 128:(tc + 1) * 128, :], s_al)
                dma(SY, gt2[b][:, :], gate_sc[tc * 128:(tc + 1) * 128, :], s_al)
                W(SC, s_al)
                W(SC, s_dve, ld_war[b])
                isq = SC.activation(dummy2[:, :], ab2[b][:, :], AF.Square,
                                    accum_out=ss2[:, tc:tc + 1])
                s_act.inc(isq)
                W(SC, s_act)
                i3 = SC.activation(ss2[:, tc:tc + 1], ss2[:, tc:tc + 1], AF.Sqrt,
                                   bias=t_const[:, 0:1], scale=t_const[:, 1:2])
                s_act.inc(i3)
                W(VE, s_act)
                iv = VE.reciprocal(ss2[:, tc:tc + 1], ss2[:, tc:tc + 1])
                s_dve.inc(iv)
                W(VE, s_dve)
                W(VE, s_pe, p6_prod_pe[0])
                VE.tensor_tensor(prod_tm[b][:, :], ab2[b][:, :], gt2[b][:, :],
                                 OP.mult)
                iv = VE.tensor_scalar(prod_tm[b][:, :], prod_tm[b][:, :],
                                      ss2[:, tc:tc + 1], None, OP.mult)
                ld_war[b] = s_dve.inc(iv)
                prev_mark[0] = ld_war[b]
                W(PE, s_dve, ld_war[b])
                for cg in range(8):
                    bk = cg % 2
                    W(PE, s_dve, bank_war6[bk])
                    it = None
                    for ci in range(4):
                        c = cg * 4 + ci
                        it = PE.transpose(psbf01[bk][:, ci * 128:(ci + 1) * 128],
                                          prod_tm[b][:, c * 128:(c + 1) * 128],
                                          t_eye[:, :])
                    mtr = s_pe.inc(it)
                    W(VE, s_pe, mtr)
                    iv = None
                    for ci in range(4):
                        c = cg * 4 + ci
                        iv = VE.tensor_copy(
                            t_prod[:, c * T_OWN + tc * 128:c * T_OWN + (tc + 1) * 128],
                            psbf01[bk][:, ci * 128:(ci + 1) * 128])
                    bank_war6[bk] = s_dve.inc(iv)
                p6_prod_pe[0] = s_pe.v
            prod_mark = s_dve.v

            war_w6 = [0, 0]
            nrm_w6 = [0, 0]
            war_sg = [0, 0]
            sg_spill = [0, 0]
            for ofc in range(32):
                b = ofc % 2
                W(GP, s_pe, war_w6[b])
                dma(GP, wob[b][:, :], wout[ofc], s_wl)
                wl_mark = s_wl.v
                W(VE, s_wl, wl_mark)
                W(VE, s_pe, war_w6[b])
                ivn = None
                for c in range(32):
                    ivn = VE.tensor_scalar(wob[b][:, c * 128:(c + 1) * 128],
                                           wob[b][:, c * 128:(c + 1) * 128],
                                           t_norm[:, c:c + 1], None, OP.mult)
                nrm_w6[b] = s_dve.inc(ivn)
                banks = (2, 3) if b == 0 else (4, 5)
                W(PE, s_dve, prod_mark)
                W(PE, s_dve, nrm_w6[b])
                W(PE, s_dve, war_sg[b])
                mm = None
                for c in range(32):
                    for th in range(NTH):
                        mm = PE.matmul(psb[banks[th]][:, :],
                                       wob[b][:, c * 128:(c + 1) * 128],
                                       t_prod[:, c * T_OWN + th * 512:c * T_OWN + (th + 1) * 512],
                                       start=(c == 0), stop=(c == 31))
                war_w6[b] = s_pe.inc(mm)
                W(VE, s_pe, war_w6[b])
                W(VE, s_st, sg_spill[b])
                iv = None
                for th in range(NTH):
                    iv = VE.tensor_tensor(
                        sgt[b][:, th * 512:(th + 1) * 512],
                        psb[banks[th]][:, :],
                        t_gfm[:, ofc * T_OWN + th * 512:ofc * T_OWN + (th + 1) * 512],
                        OP.mult)
                war_sg[b] = s_dve.inc(iv)
                W(SY, s_dve, war_sg[b])
                sg_spill[b] = dma(SY, sg_sc[ofc * 128:(ofc + 1) * 128, :],
                                  sgt[b][:, :], s_st)
            sg_done = s_st.v

            for tc in range(NTC):
                W(SY, s_out)
                W(SY, s_st, sg_done)
                dma(SY, xb2[:, :], x_own[tc * 128:(tc + 1) * 128, :], s_al)
                dma(SY, sgc[:, :],
                    sg_sc[:, tc * 128:(tc + 1) * 128]
                    .rearrange("(c p) t -> p c t", p=128), s_al)
                W(PE, s_al)
                for cg in range(8):
                    bk = cg % 2
                    W(PE, s_dve, bank_war6[bk])
                    it = None
                    for ci in range(4):
                        c = cg * 4 + ci
                        it = PE.transpose(psbf01[bk][:, ci * 128:(ci + 1) * 128],
                                          sgc[:, c * 128:(c + 1) * 128],
                                          t_eye[:, :])
                    mtr = s_pe.inc(it)
                    W(VE, s_pe, mtr)
                    iv = None
                    for ci in range(4):
                        c = cg * 4 + ci
                        iv = VE.tensor_tensor(
                            xb2[:, (cg * 4 + ci) * 128:(cg * 4 + ci + 1) * 128],
                            xb2[:, (cg * 4 + ci) * 128:(cg * 4 + ci + 1) * 128],
                            psbf01[bk][:, ci * 128:(ci + 1) * 128], OP.add)
                    bank_war6[bk] = s_dve.inc(iv)
                W(SY, s_dve)
                dma(SY, y[tc * 128:(tc + 1) * 128, :], xb2[:, :], s_out)
            if DEBUG:
                barrier()
                dma(GP, dbg["d_xn"], xn_own_sc, s_out)
                dma(GP, dbg["d_qkv"], qkv_sc[:, S_PRE:S_PRE + T_OWN], s_out)
                dma(GP, dbg["d_attn"], attn_sc, s_out)
                dma(GP, dbg["d_gate"], gate_sc, s_out)
                dma(GP, dbg["d_h"], h_sc, s_out)
                dma(GP, dbg["d_sg"], sg_sc, s_out)
                dma(GP, dbg["d_rstd"], rstd_sc, s_out)
                W(GP, s_out)
            W(SY, s_out)
        st56.close()
    return nc


def kernel(**inputs):
    per_core = _prep(inputs)
    nc = build_nc()
    res = run_bass_kernel_spmd(nc, per_core, core_ids=list(range(8)))
    global LAST_RESULTS
    LAST_RESULTS = res
    y = np.zeros((B, SEQ, HID), dtype=np.float32)
    for c in range(8):
        s, p = c // 4, c % 4
        y[s, p * T_OWN:(p + 1) * T_OWN] = res.results[c]["y"]
    return y
